# version 02
# baseline (speedup 1.0000x reference)
import numpy as np
from contextlib import ExitStack

import concourse.bass as bass
import concourse.mybir as mybir
from concourse.bass_utils import run_bass_kernel_spmd

F32 = mybir.dt.float32
BF16 = mybir.dt.bfloat16
AF = mybir.ActivationFunctionType
OP = mybir.AluOpType

D = 2048
T = 8192
NC = 8
TPC = 1024
BLK = 512
NH = 16
HD = 128
DQKV = 3 * D
KC = D // 128
TC = TPC // 128
MAGIC = 12582912.0
NEG_BIG = -1e30
LN_EPS = 1e-5

CORE_BLOCKS = [(i, 15 - i) for i in range(NC)]


class Sem:

    def __init__(self, nc, ctx, name):
        self.h = ctx.enter_context(nc.semaphore(name))
        self.n = 0

    def inc(self, inst, k=1):
        inst.then_inc(self.h, k)
        self.n += k

    def wait(self, eng, v=None):
        eng.wait_ge(self.h, self.n if v is None else v)


def make_identity(nc, identity, sem=None):
    inst = nc.gpsimd.memset(identity, 0.0)
    if sem is not None:
        sem.inc(inst)
        sem.wait(nc.gpsimd)
    inst = nc.gpsimd.affine_select(
        out=identity, in_=identity,
        compare_op=OP.not_equal, fill=1.0,
        base=0, pattern=[[-1, identity.shape[1]]], channel_multiplier=1,
    )
    if sem is not None:
        sem.inc(inst)


def emit_layernorm_tm(nc, block, x_tm, scr, ln_tm, s_in, s_done, out_dtype_note=None):
    sq, msum, mean, sumsq, m2, var, rstd0, rstd = (
        scr[k] for k in ("sq", "msum", "mean", "sumsq", "m2", "var", "rstd0", "rstd"))
    s_mid = scr["s_mid"]

    @block.scalar
    def _(s):
        s_in.wait(s)
        inst = s.activation(sq[:, :], x_tm[:, :], AF.Square, accum_out=sumsq[:, :])
        s_mid.inc(inst)

    @block.vector
    def _(v):
        s_in.wait(v)
        inst = v.tensor_reduce(msum[:, :], x_tm[:, :], axis=mybir.AxisListType.X, op=OP.add)
        s_mid.inc(inst)
        s_mid.wait(v)
        inst = v.tensor_scalar_mul(mean[:, :], msum[:, :], 1.0 / D)
        s_mid.inc(inst)
        s_mid.wait(v)
        inst = v.tensor_mul(m2[:, :], mean[:, :], mean[:, :])
        s_mid.inc(inst)
        s_mid.wait(v)
        inst = v.scalar_tensor_tensor(out=var[:, :], in0=sumsq[:, :], scalar=1.0 / D,
                                      in1=m2[:, :], op0=OP.mult, op1=OP.subtract)
        s_mid.inc(inst)

    @block.scalar
    def _(s):
        s_mid.wait(s)
        inst = s.activation(rstd0[:, :], var[:, :], AF.Sqrt, bias=scr["eps_col"][:, :])
        s_mid.inc(inst)

    @block.vector
    def _(v):
        s_mid.wait(v)
        inst = v.reciprocal(rstd[:, :], rstd0[:, :])
        s_mid.inc(inst)
        s_mid.wait(v)
        inst = v.tensor_scalar(out=ln_tm[:, :], in0=x_tm[:, :],
                               scalar1=mean[:, :], scalar2=rstd[:, :],
                               op0=OP.subtract, op1=OP.mult)
        s_done.inc(inst)



def build_A(debug_ln=False):
    nc = bass.Bass("TRN2", target_bir_lowering=False)
    x_in = nc.dram_tensor("x_in", [TPC, D], F32, kind="ExternalInput").ap()
    w_qkv = nc.dram_tensor("w_qkv", [D, DQKV], F32, kind="ExternalInput").ap()
    b_qkv = nc.dram_tensor("b_qkv", [DQKV], F32, kind="ExternalInput").ap()

    q_fm = nc.dram_tensor("q_fm", [D, TPC], BF16, kind="ExternalOutput").ap()
    k_fm = nc.dram_tensor("k_fm", [D, TPC], BF16, kind="ExternalOutput").ap()
    v_tm = nc.dram_tensor("v_tm", [NH, TPC, HD], BF16, kind="ExternalOutput").ap()
    x_fm = nc.dram_tensor("x_fm", [D, TPC], F32, kind="ExternalOutput").ap()
    qn2 = nc.dram_tensor("qn2", [NH, TPC], F32, kind="ExternalOutput").ap()
    kn2 = nc.dram_tensor("kn2", [NH, TPC], F32, kind="ExternalOutput").ap()

    with ExitStack() as ctx:
        ln_fm = ctx.enter_context(nc.sbuf_tensor("ln_fm", [128, KC * TPC], BF16))
        ident = ctx.enter_context(nc.sbuf_tensor("ident", [128, 128], BF16))
        identf = ctx.enter_context(nc.sbuf_tensor("identf", [128, 128], F32))
        ones1 = ctx.enter_context(nc.sbuf_tensor("ones1", [1, 512], BF16))
        onesc = ctx.enter_context(nc.sbuf_tensor("onesc", [128, 1], BF16))
        brow = ctx.enter_context(nc.sbuf_tensor("brow", [1, DQKV], BF16))
        x_tm = ctx.enter_context(nc.sbuf_tensor("x_tm", [128, D], F32))
        ln_tm = ctx.enter_context(nc.sbuf_tensor("ln_tm", [128, D], BF16))
        xf_ev = ctx.enter_context(nc.sbuf_tensor("xf_ev", [128, 512], F32))
        w_o = [ctx.enter_context(nc.sbuf_tensor(f"w_o{i}", [128, KC * 128], BF16)) for i in range(2)]
        ev = ctx.enter_context(nc.sbuf_tensor("ev", [128, TPC], BF16))
        evsq = ctx.enter_context(nc.sbuf_tensor("evsq", [128, 512], BF16))
        n2r = ctx.enter_context(nc.sbuf_tensor("n2r", [1, TPC], F32))
        vt_ev = ctx.enter_context(nc.sbuf_tensor("vt_ev", [128, TC * HD], BF16))
        scr = {
            "sq": ctx.enter_context(nc.sbuf_tensor("sq", [128, D], F32)),
            "msum": ctx.enter_context(nc.sbuf_tensor("msum", [128, 1], F32)),
            "mean": ctx.enter_context(nc.sbuf_tensor("mean", [128, 1], F32)),
            "sumsq": ctx.enter_context(nc.sbuf_tensor("sumsq", [128, 1], F32)),
            "m2": ctx.enter_context(nc.sbuf_tensor("m2", [128, 1], F32)),
            "var": ctx.enter_context(nc.sbuf_tensor("var", [128, 1], F32)),
            "rstd0": ctx.enter_context(nc.sbuf_tensor("rstd0", [128, 1], F32)),
            "rstd": ctx.enter_context(nc.sbuf_tensor("rstd", [128, 1], F32)),
            "eps_col": ctx.enter_context(nc.sbuf_tensor("eps_col", [128, 1], F32)),
        }
        tp = ctx.enter_context(nc.psum_tensor("tp", [128, 512], BF16))
        tpf = ctx.enter_context(nc.psum_tensor("tpf", [128, 512], F32))
        mm = [ctx.enter_context(nc.psum_tensor(f"mm{i}", [128, 512], F32)) for i in range(2)]
        n2p = ctx.enter_context(nc.psum_tensor("n2p", [1, 512], F32))

        s_pre = Sem(nc, ctx, "s_pre")
        s_x = Sem(nc, ctx, "s_x")
        s_ln = Sem(nc, ctx, "s_ln")
        scr["s_mid"] = Sem(nc, ctx, "s_mid")
        s_tr = Sem(nc, ctx, "s_tr")
        s_ev = Sem(nc, ctx, "s_ev")
        s_w = Sem(nc, ctx, "s_w")
        s_mm = Sem(nc, ctx, "s_mm")
        s_ev2 = Sem(nc, ctx, "s_ev2")
        s_sq = Sem(nc, ctx, "s_sq")
        s_n2 = Sem(nc, ctx, "s_n2")
        s_out = Sem(nc, ctx, "s_out")
        block = ctx.enter_context(nc.Block())

        @block.gpsimd
        def _(g):
            make_identity(nc, ident[:, :], s_pre)
            make_identity(nc, identf[:, :], s_pre)
            s_pre.inc(g.memset(ones1[:, :], 1.0))
            s_pre.inc(g.memset(onesc[:, :], 1.0))
            s_pre.inc(g.memset(scr["eps_col"][:, :], LN_EPS))
            s_pre.inc(g.dma_start(out=brow[:, :], in_=b_qkv.rearrange("(a n) -> a n", a=1)), 16)

        for c in range(TC):
            @block.sync
            def _(s, c=c):
                s_tr.wait(s)
                inst = s.dma_start(out=x_tm[:, :], in_=x_in[c * 128:(c + 1) * 128, :])
                s_x.inc(inst, 16)

            if c == 0:
                @block.scalar
                def _(s):
                    s_pre.wait(s)
                @block.vector
                def _(v):
                    s_pre.wait(v)
            emit_layernorm_tm(nc, block, x_tm, scr, ln_tm, s_x, s_ln)

            for grp in range(4):
                @block.tensor
                def _(t, c=c, grp=grp):
                    s_pre.wait(t)
                    s_ln.wait(t)
                    s_ev.wait(t)
                    for j in range(4):
                        dd = grp * 4 + j
                        t.transpose(tp[:, j * 128:(j + 1) * 128],
                                    ln_tm[:, dd * 128:(dd + 1) * 128], ident[:, :])
                        inst = t.transpose(tpf[:, j * 128:(j + 1) * 128],
                                           x_tm[:, dd * 128:(dd + 1) * 128], identf[:, :])
                        if j == 3:
                            s_tr.inc(inst)

                @block.scalar
                def _(s, c=c, grp=grp):
                    s_tr.wait(s)
                    s_out.wait(s)
                    for j in range(4):
                        dd = grp * 4 + j
                        s.activation(ln_fm[:, dd * TPC + c * 128: dd * TPC + (c + 1) * 128],
                                     tp[:, j * 128:(j + 1) * 128], AF.Identity)
                    inst = s.activation(xf_ev[:, :], tpf[:, :], AF.Identity)
                    s_ev.inc(inst)

                @block.sync
                def _(s, c=c, grp=grp):
                    s_ev.wait(s)
                    out_ap = x_fm.rearrange("(g p) t -> p g t", p=128)[:, grp * 4:(grp + 1) * 4,
                                                                       c * 128:(c + 1) * 128]
                    src_ap = xf_ev[:, :].rearrange("p (g t) -> p g t", g=4)
                    inst = s.dma_start(out=out_ap, in_=src_ap)
                    s_out.inc(inst, 16)

            @block.sync
            def _(s, c=c):
                s_ev.wait(s)

        OC = DQKV // 128
        qkv_mm_done = {}
        for o in range(OC):
            buf = o % 2

            @block.gpsimd
            def _(g, o=o, buf=buf):
                if o >= 2:
                    s_mm.wait(g, qkv_mm_done[o - 2])
                for q in range(4):
                    src = w_qkv.rearrange("(kc p) o -> p kc o", p=128)[
                        :, q * 4:(q + 1) * 4, o * 128:(o + 1) * 128]
                    dst = w_o[buf][:, :].rearrange("p (kc o) -> p kc o", kc=KC)[
                        :, q * 4:(q + 1) * 4, :]
                    inst = g.dma_start(out=dst, in_=src)
                    s_w.inc(inst, 16)

            for tb in range(2):
                @block.tensor
                def _(t, o=o, buf=buf, tb=tb):
                    s_w.wait(t)
                    s_ev.wait(t)
                    if o == 0 and tb == 0:
                        s_pre.wait(t)
                    s_ev2.wait(t)
                    for k in range(KC):
                        t.matmul(mm[tb][:, :], w_o[buf][:, k * 128:(k + 1) * 128],
                                 ln_fm[:, k * TPC + tb * 512: k * TPC + tb * 512 + 512],
                                 start=(k == 0), stop=False)
                    inst = t.matmul(mm[tb][:, :], brow[0:1, o * 128:(o + 1) * 128],
                                    ones1[0:1, :], start=False, stop=True)
                    s_mm.inc(inst)
                    if tb == 1:
                        qkv_mm_done[o] = s_mm.n

                @block.scalar
                def _(s, o=o, tb=tb):
                    s_mm.wait(s)
                    s_out.wait(s)
                    s_n2.wait(s)
                    s_tr.wait(s)
                    inst = s.activation(ev[:, tb * 512:(tb + 1) * 512], mm[tb][:, :], AF.Identity)
                    s_ev2.inc(inst)

                if o < 32:
                    @block.scalar
                    def _(s, o=o, tb=tb):
                        s_ev2.wait(s)
                        inst = s.activation(evsq[:, :], ev[:, tb * 512:(tb + 1) * 512], AF.Square)
                        s_sq.inc(inst)

                    @block.tensor
                    def _(t, o=o, tb=tb):
                        s_sq.wait(t)
                        s_n2.wait(t)
                        inst = t.matmul(n2p[:, :], onesc[:, :], evsq[:, :], start=True, stop=True)
                        s_mm.inc(inst)

                    @block.vector
                    def _(v, o=o, tb=tb):
                        s_mm.wait(v)
                        inst = v.tensor_copy(n2r[:, tb * 512:(tb + 1) * 512], n2p[:, :])
                        s_n2.inc(inst)

            if o < 16:
                @block.sync
                def _(s, o=o):
                    s_ev2.wait(s)
                    inst = s.dma_start(out=q_fm[o * 128:(o + 1) * 128, :], in_=ev[:, :])
                    s_out.inc(inst, 16)
                    s_n2.wait(s)
                    inst = s.dma_start(out=qn2[o:o + 1, :], in_=n2r[:, :])
                    s_out.inc(inst, 16)
            elif o < 32:
                @block.sync
                def _(s, o=o):
                    s_ev2.wait(s)
                    inst = s.dma_start(out=k_fm[(o - 16) * 128:(o - 15) * 128, :], in_=ev[:, :])
                    s_out.inc(inst, 16)
                    s_n2.wait(s)
                    inst = s.dma_start(out=kn2[o - 16:o - 15, :], in_=n2r[:, :])
                    s_out.inc(inst, 16)
            else:
                h = o - 32
                for grp in range(2):
                    @block.tensor
                    def _(t, o=o, grp=grp):
                        s_ev2.wait(t)
                        s_ev.wait(t)
                        for j in range(4):
                            cc = grp * 4 + j
                            inst = t.transpose(tp[:, j * 128:(j + 1) * 128],
                                               ev[:, cc * 128:(cc + 1) * 128], ident[:, :])
                        s_tr.inc(inst)

                    @block.scalar
                    def _(s, grp=grp):
                        s_tr.wait(s)
                        s_out.wait(s)
                        inst = s.activation(vt_ev[:, grp * 512:(grp + 1) * 512], tp[:, :], AF.Identity)
                        s_ev.inc(inst)

                @block.sync
                def _(s, h=h):
                    s_ev.wait(s)
                    out_ap = v_tm.rearrange("h t n -> (h t) n")[h * TPC:(h + 1) * TPC, :] \
                                 .rearrange("(c p) n -> p c n", p=128)
                    src_ap = vt_ev[:, :].rearrange("p (c n) -> p c n", c=TC)
                    inst = s.dma_start(out=out_ap, in_=src_ap)
                    s_out.inc(inst, 16)

        if debug_ln:
            ln_dbg = nc.dram_tensor("ln_dbg", [128, KC * TPC], BF16, kind="ExternalOutput").ap()
            st_dbg = nc.dram_tensor("st_dbg", [128, 4], F32, kind="ExternalOutput").ap()
            @block.sync
            def _(s):
                s_ev.wait(s)
                inst = s.dma_start(out=ln_dbg, in_=ln_fm[:, :])
                s_out.inc(inst, 16)
                for i, tile in enumerate([scr["mean"], scr["sumsq"], scr["var"], scr["rstd"]]):
                    inst = s.dma_start(out=st_dbg[:, i:i+1], in_=tile[:, :], allow_slow_non_contiguous=True)
                    s_out.inc(inst, 16)

        @block.sync
        def _(s):
            s_out.wait(s)

    return nc


_BUILT = {}


def _get(name, builder):
    if name not in _BUILT:
        _BUILT[name] = builder()
    return _BUILT[name]



def _blk_owner(g):
    return (g, 0) if g < NC else (15 - g, 1)


def _col(g):
    r, pos = _blk_owner(g)
    return r * TPC + pos * BLK


def _kcol(kc):
    return _col(kc // 4) + (kc % 4) * 128


def build_B():
    nc = bass.Bass("TRN2", target_bir_lowering=False)
    from concourse.bass import ds
    q_g = nc.dram_tensor("q_g", [NC * D, TPC], BF16, kind="ExternalInput").ap()
    k_g = nc.dram_tensor("k_g", [NC * D, TPC], BF16, kind="ExternalInput").ap()
    v_g = nc.dram_tensor("v_g", [NC * NH * TPC, HD], BF16, kind="ExternalInput").ap()
    qn2_g = nc.dram_tensor("qn2_g", [NC * NH, TPC], F32, kind="ExternalInput").ap()
    kn2_g = nc.dram_tensor("kn2_g", [NC * NH, TPC], F32, kind="ExternalInput").ap()
    x_in = nc.dram_tensor("x_fm", [D, TPC], F32, kind="ExternalInput").ap()
    w_proj = nc.dram_tensor("w_proj", [D, D], F32, kind="ExternalInput").ap()
    b_proj = nc.dram_tensor("b_proj", [D], F32, kind="ExternalInput").ap()
    x1_fm = nc.dram_tensor("x1_fm", [D, TPC], F32, kind="ExternalOutput").ap()

    o_cont = nc.dram_tensor("o_cont", [2 * HD, T], BF16, kind="Internal").ap()
    o_gath = nc.dram_tensor("o_gath", [NC * 2 * HD, T], BF16, kind="Internal").ap()

    SCL = 1.0 / float(np.sqrt(HD))

    with ExitStack() as ctx:
        k_my = ctx.enter_context(nc.sbuf_tensor("k_my", [128, T], BF16))
        v_my = ctx.enter_context(nc.sbuf_tensor("v_my", [128, T], BF16))
        q_my = ctx.enter_context(nc.sbuf_tensor("q_my", [128, T], BF16))
        p_sb = ctx.enter_context(nc.sbuf_tensor("p_sb", [128, 2 * 512], BF16))
        o_ev = ctx.enter_context(nc.sbuf_tensor("o_ev", [128, 512], BF16))
        masks = ctx.enter_context(nc.sbuf_tensor("masks", [128, 4 * 512], BF16))
        ones_r = ctx.enter_context(nc.sbuf_tensor("ones_r", [1, 512], BF16))
        onesc = ctx.enter_context(nc.sbuf_tensor("onesc", [128, 1], BF16))
        qn2r = ctx.enter_context(nc.sbuf_tensor("qn2r", [1, T], F32))
        kn2r = ctx.enter_context(nc.sbuf_tensor("kn2r", [1, T], F32))
        cA = ctx.enter_context(nc.sbuf_tensor("cA", [1, 8], F32))
        negc_b = ctx.enter_context(nc.sbuf_tensor("negc_b", [1, 1], BF16))
        negc_col = ctx.enter_context(nc.sbuf_tensor("negc_col", [128, 1], F32))
        rl_f = ctx.enter_context(nc.sbuf_tensor("rl_f", [1, 512], F32))
        rl_b = ctx.enter_context(nc.sbuf_tensor("rl_b", [1, 512], BF16))
        rlb = ctx.enter_context(nc.sbuf_tensor("rlb", [128, 512], F32))
        attn = ctx.enter_context(nc.sbuf_tensor("attn", [128, KC * TPC], BF16))
        brow = ctx.enter_context(nc.sbuf_tensor("brow", [1, D], BF16))
        w_o = [ctx.enter_context(nc.sbuf_tensor(f"w_o{i}", [128, KC * 128], BF16)) for i in range(2)]
        x_sb = ctx.enter_context(nc.sbuf_tensor("x_sb", [128, TPC], F32))
        o_sb = ctx.enter_context(nc.sbuf_tensor("o_sb", [128, TPC], F32))

        sp = [ctx.enter_context(nc.psum_tensor(f"sp{i}", [128, 512], F32)) for i in range(2)]
        av = ctx.enter_context(nc.psum_tensor("av", [128, 512], F32))
        lp = ctx.enter_context(nc.psum_tensor("lp", [1, 512], F32))
        ncp = ctx.enter_context(nc.psum_tensor("ncp", [128, 1], F32))
        rlp = ctx.enter_context(nc.psum_tensor("rlp", [128, 512], F32))
        mm = [ctx.enter_context(nc.psum_tensor(f"mm{i}", [128, 512], F32)) for i in range(2)]

        s_pre = Sem(nc, ctx, "s_pre")
        s_kv = Sem(nc, ctx, "s_kv")
        s_sp = Sem(nc, ctx, "s_sp")
        s_p = Sem(nc, ctx, "s_p")
        s_m = Sem(nc, ctx, "s_m")
        s_av = Sem(nc, ctx, "s_av")
        s_c = Sem(nc, ctx, "s_c")
        s_o1 = Sem(nc, ctx, "s_o1")
        s_cc = Sem(nc, ctx, "s_cc")
        s_attn = Sem(nc, ctx, "s_attn")
        s_w = Sem(nc, ctx, "s_w")
        s_mm = Sem(nc, ctx, "s_mm")
        s_x = Sem(nc, ctx, "s_x")
        s_o = Sem(nc, ctx, "s_o")
        s_out = Sem(nc, ctx, "s_out")
        block = ctx.enter_context(nc.Block())

        @block.gpsimd
        def _(g):
            s_pre.inc(g.memset(ones_r[:, :], 1.0))
            s_pre.inc(g.memset(onesc[:, :], 1.0))
            for j in range(4):
                inst = g.memset(masks[:, j * 512:(j + 1) * 512], 1.0)
                s_pre.inc(inst)
                s_pre.wait(g)
                inst = g.affine_select(
                    out=masks[:, j * 512:(j + 1) * 512],
                    in_=masks[:, j * 512:(j + 1) * 512],
                    compare_op=OP.is_ge, fill=0.0,
                    base=-128 * j, channel_multiplier=-1, pattern=[[1, 512]])
                s_pre.inc(inst)
            s_pre.inc(g.dma_start(out=brow[:, :], in_=b_proj.rearrange("(a n) -> a n", a=1)), 16)

        for slot in range(2):
            @block.gpsimd
            def _(g, slot=slot):
                pid = g.snap(g.partition_id(), min_val=0, max_val=NC - 1)
                hh = pid * 2 + slot
                s_av.wait(g)
                s_p.wait(g)
                ksrc = k_g.rearrange("(r hp p) t -> p r hp t", p=128, r=NC)
                inst = g.dma_start(
                    out=k_my[:, :].rearrange("p (r t) -> p r t", r=NC),
                    in_=ksrc[:, :, ds(hh, 1), :].rearrange("p r a t -> p (r a) t"))
                s_kv.inc(inst, 16)
                qsrc = q_g.rearrange("(r hp p) t -> p r hp t", p=128, r=NC)
                inst = g.dma_start(
                    out=q_my[:, :].rearrange("p (r t) -> p r t", r=NC),
                    in_=qsrc[:, :, ds(hh, 1), :].rearrange("p r a t -> p (r a) t"))
                s_kv.inc(inst, 16)
                vsrc = v_g.rearrange("(r hc c p) n -> p r hc c n", p=128, r=NC, c=TC)
                vslc = vsrc[:, :, ds(hh, 1), :, :].rearrange("p r a c n -> p (r a) c n")
                vdst = v_my[:, :].rearrange("p (r c n) -> p r c n", r=NC, c=TC)
                for r in range(NC):
                    inst = g.dma_start(
                        out=vdst[:, r, :, :] if False else
                            v_my[:, r * TPC:(r + 1) * TPC].rearrange("p (c n) -> p c n", c=TC),
                        in_=vslc[:, r:r + 1, :, :].rearrange("p a c n -> p (a c) n"))
                    s_kv.inc(inst, 16)

            @block.sync
            def _(s, slot=slot):
                pid = s.snap(s.partition_id(), min_val=0, max_val=NC - 1)
                hh = pid * 2 + slot
                s_c.wait(s)
                qsrc = qn2_g.rearrange("(r h) t -> h r t", h=NH)
                inst = s.dma_start(
                    out=qn2r[:, :].rearrange("a (r t) -> a r t", r=NC),
                    in_=qsrc[ds(hh, 1), :, :])
                s_kv.inc(inst, 16)
                ksrc = kn2_g.rearrange("(r h) t -> h r t", h=NH)
                inst = s.dma_start(
                    out=kn2r[:, :].rearrange("a (r t) -> a r t", r=NC),
                    in_=ksrc[ds(hh, 1), :, :])
                s_kv.inc(inst, 16)

            kv_ready = s_kv.n

            @block.vector
            def _(v, slot=slot):
                s_pre.wait(v)
                s_kv.wait(v, kv_ready)
                s_attn.wait(v)
                inst = v.reduce_max(cA[:, 1:2], kn2r[:, :], axis=mybir.AxisListType.X)
                s_c.inc(inst)

            @block.scalar
            def _(s, slot=slot):
                s_c.wait(s)
                inst = s.activation(cA[:, 3:4], cA[:, 1:2], AF.Sqrt)
                s_c.inc(inst)

            for gq in range(16):
                q0 = _col(gq)
                nkc = (gq + 1) * 4

                @block.vector
                def _(v, slot=slot, gq=gq, q0=q0):
                    s_kv.wait(v, kv_ready)
                    s_attn.wait(v)
                    inst = v.reduce_max(cA[:, 0:1], qn2r[:, q0:q0 + 512],
                                        axis=mybir.AxisListType.X)
                    s_c.inc(inst)

                @block.scalar
                def _(s, slot=slot, gq=gq):
                    s_c.wait(s)
                    inst = s.activation(cA[:, 2:3], cA[:, 0:1], AF.Sqrt)
                    s_c.inc(inst)

                @block.vector
                def _(v, slot=slot, gq=gq):
                    s_c.wait(v)
                    inst = v.tensor_mul(cA[:, 4:5], cA[:, 2:3], cA[:, 3:4])
                    s_c.inc(inst)
                    s_c.wait(v)
                    inst = v.tensor_scalar_mul(cA[:, 5:6], cA[:, 4:5], -SCL)
                    s_c.inc(inst)
                    s_c.wait(v)
                    inst = v.tensor_copy(negc_b[:, :], cA[:, 5:6])
                    s_c.inc(inst)

                @block.tensor
                def _(t, slot=slot, gq=gq):
                    s_c.wait(t)
                    inst = t.matmul(ncp[:, :], ones_r[0:1, 0:128], negc_b[:, :],
                                    start=True, stop=True)
                    s_c.inc(inst)

                @block.scalar
                def _(s, slot=slot, gq=gq):
                    s_c.wait(s)
                    inst = s.activation(negc_col[:, :], ncp[:, :], AF.Identity)
                    s_c.inc(inst)

                b_p = s_p.n
                b_av = s_av.n
                b_m = s_m.n
                b_attn = s_attn.n
                diag0 = nkc - 4

                def p_ready(kc, nkc=nkc, b_p=b_p, b_m=b_m, diag0=diag0):
                    if kc >= diag0:
                        return s_m.h, b_m + (kc - diag0) + 1
                    return s_p.h, b_p + kc + 1

                def emit_av(t, kc, nkc=nkc, p_ready=p_ready):
                    hh, vv = p_ready(kc)
                    t.wait_ge(hh, vv)
                    vcol = _kcol(kc)
                    inst = t.matmul(av[:, :], v_my[:, vcol:vcol + 128],
                                    p_sb[:, (kc % 2) * 512:(kc % 2 + 1) * 512],
                                    start=(kc == 0), stop=(kc == nkc - 1))
                    s_av.inc(inst)
                    inst = t.matmul(lp[:, :], onesc[:, :],
                                    p_sb[:, (kc % 2) * 512:(kc % 2 + 1) * 512],
                                    start=(kc == 0), stop=(kc == nkc - 1))
                    s_av.inc(inst)

                @block.tensor
                def _(t, slot=slot, gq=gq, q0=q0, nkc=nkc, emit_av=emit_av,
                      b_p=b_p, b_attn=b_attn, kv_ready=kv_ready):
                    s_kv.wait(t, kv_ready)
                    s_attn.wait(t, b_attn)
                    s_c.wait(t)
                    for kc in range(nkc):
                        t.wait_ge(s_p.h, b_p + max(0, kc - 1))
                        kcol = _kcol(kc)
                        inst = t.matmul(sp[kc % 2][:, :],
                                        k_my[:, kcol:kcol + 128],
                                        q_my[:, q0:q0 + 512],
                                        start=True, stop=True)
                        s_sp.inc(inst)
                        if kc >= 1:
                            emit_av(t, kc - 1)
                    emit_av(t, nkc - 1)

                @block.scalar
                def _(s, slot=slot, gq=gq, nkc=nkc, b_av=b_av):
                    s_c.wait(s)
                    b_sp = s_sp.n - nkc
                    for kc in range(nkc):
                        s.wait_ge(s_sp.h, b_sp + kc + 1)
                        s.wait_ge(s_av.h, b_av + 2 * (kc - 1) if kc >= 2 else b_av)
                        inst = s.activation(p_sb[:, (kc % 2) * 512:(kc % 2 + 1) * 512],
                                            sp[kc % 2][:, :], AF.Exp,
                                            bias=negc_col[:, :], scale=SCL)
                        s_p.inc(inst)

                @block.vector
                def _(v, slot=slot, gq=gq, nkc=nkc, b_p=b_p, diag0=diag0):
                    s_pre.wait(v)
                    for kc in range(diag0, nkc):
                        v.wait_ge(s_p.h, b_p + kc + 1)
                        j_idx = kc - diag0
                        inst = v.tensor_mul(p_sb[:, (kc % 2) * 512:(kc % 2 + 1) * 512],
                                            p_sb[:, (kc % 2) * 512:(kc % 2 + 1) * 512],
                                            masks[:, j_idx * 512:(j_idx + 1) * 512])
                        s_m.inc(inst)

                @block.vector
                def _(v, slot=slot, gq=gq):
                    s_av.wait(v)
                    inst = v.reciprocal(rl_f[:, :], lp[:, :])
                    s_c.inc(inst)
                    s_c.wait(v)
                    inst = v.tensor_copy(rl_b[:, :], rl_f[:, :])
                    s_c.inc(inst)

                @block.tensor
                def _(t, slot=slot, gq=gq):
                    s_c.wait(t)
                    inst = t.matmul(rlp[:, :], ones_r[0:1, 0:128], rl_b[:, :],
                                    start=True, stop=True)
                    s_c.inc(inst)

                @block.scalar
                def _(s, slot=slot, gq=gq):
                    s_c.wait(s)
                    inst = s.activation(rlb[:, :], rlp[:, :], AF.Identity)
                    s_c.inc(inst)

                @block.vector
                def _(v, slot=slot, gq=gq):
                    s_c.wait(v)
                    s_o1.wait(v)
                    inst = v.tensor_mul(o_ev[:, :], av[:, :], rlb[:, :])
                    s_attn.inc(inst)

                @block.sync
                def _(s, slot=slot, gq=gq, q0=q0):
                    s_attn.wait(s)
                    inst = s.dma_start(out=o_cont[slot * HD:(slot + 1) * HD, q0:q0 + 512],
                                       in_=o_ev[:, :])
                    s_o1.inc(inst, 16)

        @block.gpsimd
        def _(g):
            s_o1.wait(g)
            inst = g.collective_compute(
                "AllGather", OP.bypass,
                replica_groups=[list(range(NC))],
                ins=[o_cont.opt()], outs=[o_gath.opt()])
            s_cc.inc(inst)
            g.wait_ge(s_cc.h, 1)
            pid = g.snap(g.partition_id(), min_val=0, max_val=NC - 1)
            inst = g.dma_start(
                out=attn[:, :].rearrange("p (ch t) -> p ch t", ch=KC),
                in_=o_gath.rearrange("(ch p) t -> p ch t", p=128)[:, :, ds(pid * TPC, TPC)])
            s_cc.inc(inst, 16)
        attn_ready = s_cc.n

        proj_mm_done = {}
        for o in range(KC):
            buf = o % 2

            @block.gpsimd
            def _(g, o=o, buf=buf):
                if o >= 2:
                    s_mm.wait(g, proj_mm_done[o - 2])
                for q in range(4):
                    src = w_proj.rearrange("(kc p) o -> p kc o", p=128)[
                        :, q * 4:(q + 1) * 4, o * 128:(o + 1) * 128]
                    dst = w_o[buf][:, :].rearrange("p (kc o) -> p kc o", kc=KC)[
                        :, q * 4:(q + 1) * 4, :]
                    inst = g.dma_start(out=dst, in_=src)
                    s_w.inc(inst, 16)

            @block.sync
            def _(s, o=o):
                s_o.wait(s)
                inst = s.dma_start(out=x_sb[:, :], in_=x_in[o * 128:(o + 1) * 128, :])
                s_x.inc(inst, 16)

            for tb in range(2):
                @block.tensor
                def _(t, o=o, buf=buf, tb=tb):
                    s_w.wait(t)
                    s_cc.wait(t, attn_ready)
                    if o == 0 and tb == 0:
                        s_pre.wait(t)
                    s_o.wait(t)
                    for k in range(KC):
                        t.matmul(mm[tb][:, :], w_o[buf][:, k * 128:(k + 1) * 128],
                                 attn[:, k * TPC + tb * 512: k * TPC + tb * 512 + 512],
                                 start=(k == 0), stop=False)
                    inst = t.matmul(mm[tb][:, :], brow[0:1, o * 128:(o + 1) * 128],
                                    ones_r[0:1, :], start=False, stop=True)
                    s_mm.inc(inst)
                    if tb == 1:
                        proj_mm_done[o] = s_mm.n

                @block.vector
                def _(v, o=o, tb=tb):
                    s_mm.wait(v)
                    s_x.wait(v)
                    s_out.wait(v)
                    inst = v.tensor_add(o_sb[:, tb * 512:(tb + 1) * 512], mm[tb][:, :],
                                        x_sb[:, tb * 512:(tb + 1) * 512])
                    s_o.inc(inst)

            @block.sync
            def _(s, o=o):
                s_o.wait(s)
                inst = s.dma_start(out=x1_fm[o * 128:(o + 1) * 128, :], in_=o_sb[:, :])
                s_out.inc(inst, 16)

        @block.sync
        def _(s):
            s_out.wait(s)

    return nc



H_FFN = 8192
HC = H_FFN // 128


def emit_weight_stage(nc, block, ctx, sems, srcs, tag=""):
    s_cast, s_tr, s_st = sems["cast"], sems["tr"], sems["st"]
    tr_sb = ctx.enter_context(nc.sbuf_tensor(f"trs_sb{tag}", [128, H_FFN // 4], BF16))
    sg_sb = None
    if any(sign == 1 for (_, _, sign) in srcs):
        sg_sb = ctx.enter_context(nc.sbuf_tensor(f"sg_sb{tag}", [128, H_FFN], F32))
    outs = []
    for i, (src, want_t, sign) in enumerate(srcs):
        R, C = src.shape
        w_bf = nc.dram_tensor(f"wst_bf{tag}{i}", [R, C], BF16, kind="Internal").ap()
        w_bfT = (nc.dram_tensor(f"wst_bfT{tag}{i}", [C, R], BF16, kind="Internal").ap()
                 if want_t else None)
        outs.append((w_bf, w_bfT))

        if not sign or sign == 2:
            @block.gpsimd
            def _(g, src=src, w_bf=w_bf, R=R):
                for r0 in range(0, R, 1024):
                    inst = g.dma_start(out=w_bf[r0:r0 + 1024, :], in_=src[r0:r0 + 1024, :])
                    s_cast.inc(inst, 16)
        else:
            for r0 in range(0, R, 128):
                @block.sync
                def _(s, src=src, r0=r0, C=C):
                    s_st.wait(s)
                    inst = s.dma_start(out=sg_sb[:, :C], in_=src[r0:r0 + 128, :])
                    s_cast.inc(inst, 16)

                @block.scalar
                def _(s, r0=r0, C=C):
                    s_cast.wait(s)
                    s_tr.wait(s)
                    inst = s.activation(tr_sb[:, :C], sg_sb[:, :C], AF.Sign)
                    s_tr.inc(inst)

                @block.sync
                def _(s, w_bf=w_bf, r0=r0, C=C):
                    s_tr.wait(s)
                    inst = s.dma_start(out=w_bf[r0:r0 + 128, :], in_=tr_sb[:, :C])
                    s_st.inc(inst, 16)

        if want_t:
            RH = min(R, H_FFN // 4)
            for c0 in range(0, C, 128):
                for r0 in range(0, R, RH):
                    @block.scalar
                    def _(s, w_bf=w_bf, c0=c0, r0=r0, RH=RH, sign=sign):
                        s_cast.wait(s)
                        s_st.wait(s)
                        inst = s.dma_start_transpose(tr_sb[:, :RH],
                                                     w_bf[r0:r0 + RH, c0:c0 + 128])
                        s_tr.inc(inst, 16)
                        if sign == 2:
                            s_tr.wait(s)
                            inst = s.activation(tr_sb[:, :RH], tr_sb[:, :RH], AF.Sign)
                            s_tr.inc(inst)

                    @block.sync
                    def _(s, w_bfT=w_bfT, c0=c0, r0=r0, RH=RH):
                        s_tr.wait(s)
                        inst = s.dma_start(out=w_bfT[c0:c0 + 128, r0:r0 + RH], in_=tr_sb[:, :RH])
                        s_st.inc(inst, 16)
    return outs


def emit_ln_fm(nc, block, ctx, pool, tiles, x_src, ln_sb, sems, eps, s_ready, tag="", xc=None, sem_ctx=None):
    if sem_ctx is None:
        sem_ctx = ctx
    s_e = Sem(nc, sem_ctx, f"s_lne{tag}")
    s_mmn = Sem(nc, sem_ctx, f"s_lnm{tag}")
    s_ld = Sem(nc, sem_ctx, f"s_lnl{tag}")
    if xc is None:
        xc = ctx.enter_context(nc.sbuf_tensor(f"ln_xc{tag}", [128, TPC], F32))
    emit_ln_fm.last_mr = None
    sqb = ctx.enter_context(nc.sbuf_tensor(f"ln_sqb{tag}", [128, TPC], BF16))
    m_r = ctx.enter_context(nc.sbuf_tensor(f"ln_mr{tag}", [1, TPC], F32))
    v_r = ctx.enter_context(nc.sbuf_tensor(f"ln_vr{tag}", [1, TPC], F32))
    m_b = ctx.enter_context(nc.sbuf_tensor(f"ln_mbb{tag}", [1, TPC], BF16))
    r_b = ctx.enter_context(nc.sbuf_tensor(f"ln_rbb{tag}", [1, TPC], BF16))
    mb = ctx.enter_context(nc.sbuf_tensor(f"ln_mb{tag}", [128, TPC], BF16))
    rb = ctx.enter_context(nc.sbuf_tensor(f"ln_rb{tag}", [128, TPC], BF16))
    epsr = ctx.enter_context(nc.sbuf_tensor(f"ln_eps{tag}", [1, 1], F32))
    onesc, ones_r = tiles["onesc"], tiles["ones_r"]

    @block.gpsimd
    def _(g):
        g.memset(epsr[:, :], float(eps)).then_inc(s_e.h, 1)
        s_e.n += 1

    for k in range(KC):
        @block.sync
        def _(s, k=k):
            s_e.wait(s)
            inst = s.dma_start(out=xc[:, :], in_=x_src[k * 128:(k + 1) * 128, :])
            s_ld.inc(inst, 16)

        xb = ln_sb[:, k * TPC:(k + 1) * TPC]

        @block.vector
        def _(v, k=k, xb=xb):
            s_ld.wait(v)
            s_mmn.wait(v)
            inst = v.tensor_copy(xb, xc[:, :])
            s_e.inc(inst)

        @block.scalar
        def _(s, k=k, xb=xb):
            s_e.wait(s)
            inst = s.activation(sqb[:, :], xb, AF.Square)
            s_e.inc(inst)

        @block.tensor
        def _(t, k=k, xb=xb):
            s_e.wait(t)
            for tb in range(2):
                t.matmul(pool[0 + tb][0:1, :], onesc[:, :], xb[:, tb * 512:(tb + 1) * 512],
                         start=(k == 0), stop=(k == KC - 1))
                inst = t.matmul(pool[2 + tb][0:1, :], onesc[:, :], sqb[:, tb * 512:(tb + 1) * 512],
                                start=(k == 0), stop=(k == KC - 1))
            s_mmn.inc(inst)

    @block.vector
    def _(v):
        s_mmn.wait(v)
        for tb in range(2):
            inst = v.tensor_scalar_mul(m_r[:, tb * 512:(tb + 1) * 512], pool[tb][0:1, :], 1.0 / D)
            s_e.inc(inst)
        s_e.wait(v)
        inst = v.tensor_mul(v_r[:, :], m_r[:, :], m_r[:, :])
        s_e.inc(inst)
        s_e.wait(v)
        for tb in range(2):
            inst = v.scalar_tensor_tensor(
                out=v_r[:, tb * 512:(tb + 1) * 512], in0=pool[2 + tb][0:1, :],
                scalar=1.0 / D, in1=v_r[:, tb * 512:(tb + 1) * 512],
                op0=OP.mult, op1=OP.subtract)
            s_e.inc(inst)

    @block.scalar
    def _(s):
        s_e.wait(s)
        inst = s.activation(v_r[:, :], v_r[:, :], AF.Sqrt, bias=epsr[:, :])
        s_e.inc(inst)

    @block.vector
    def _(v):
        s_e.wait(v)
        inst = v.reciprocal(v_r[:, :], v_r[:, :])
        s_e.inc(inst)
        s_e.wait(v)
        inst = v.tensor_copy(m_b[:, :], m_r[:, :])
        s_e.inc(inst)
        inst = v.tensor_copy(r_b[:, :], v_r[:, :])
        s_e.inc(inst)

    for (rowt, bcast) in ((m_b, mb), (r_b, rb)):
        for tb in range(2):
            @block.tensor
            def _(t, rowt=rowt, tb=tb):
                s_e.wait(t)
                s_mmn.wait(t)
                inst = t.matmul(pool[4][:, :], ones_r[0:1, 0:128],
                                rowt[0:1, tb * 512:(tb + 1) * 512], start=True, stop=True)
                s_mmn.inc(inst)

            @block.scalar
            def _(s, bcast=bcast, tb=tb):
                s_mmn.wait(s)
                inst = s.activation(bcast[:, tb * 512:(tb + 1) * 512], pool[4][:, :], AF.Identity)
                s_mmn.inc(inst)

    for k in range(KC):
        @block.sync
        def _(s, k=k):
            s_e.wait(s)
            inst = s.dma_start(out=xc[:, :], in_=x_src[k * 128:(k + 1) * 128, :])
            s_ld.inc(inst, 16)

        @block.vector
        def _(v, k=k):
            s_ld.wait(v)
            s_mmn.wait(v)
            inst = v.tensor_sub(xc[:, :], xc[:, :], mb[:, :])
            s_e.inc(inst)
            s_e.wait(v)
            inst = v.tensor_mul(ln_sb[:, k * TPC:(k + 1) * TPC], xc[:, :], rb[:, :])
            s_e.inc(inst)
            if k == KC - 1:
                s_e.wait(v)
                s_ready.inc(v.engine_nop())
    emit_ln_fm.last_mr = (m_r, v_r, mb, rb)


def build_C(debug=False, ln_only=False, no_down=False):
    nc = bass.Bass("TRN2", target_bir_lowering=False)
    x_in = nc.dram_tensor("x1_fm", [D, TPC], F32, kind="ExternalInput").ap()
    w_gate = nc.dram_tensor("w_gate", [H_FFN, D], F32, kind="ExternalInput").ap()
    w_up = nc.dram_tensor("w_up", [H_FFN, D], F32, kind="ExternalInput").ap()
    w_down = nc.dram_tensor("w_down", [D, H_FFN], F32, kind="ExternalInput").ap()
    x2_fm = nc.dram_tensor("x2_fm", [D, TPC], F32, kind="ExternalOutput").ap()

    with ExitStack() as ctx:
        ln_sb = ctx.enter_context(nc.sbuf_tensor("ln_sb", [128, KC * TPC], BF16))
        gu = ctx.enter_context(nc.sbuf_tensor("gu", [128, HC * TPC], BF16))
        wg_sb = [ctx.enter_context(nc.sbuf_tensor(f"wg_sb{i}", [128, KC * 128], BF16)) for i in range(2)]
        wu_sb = [ctx.enter_context(nc.sbuf_tensor(f"wu_sb{i}", [128, KC * 128], BF16)) for i in range(2)]
        wd_sb = [ctx.enter_context(nc.sbuf_tensor(f"wd_sb{i}", [128, 1024], BF16)) for i in range(2)]

        ones_r = ctx.enter_context(nc.sbuf_tensor("ones_r", [1, 512], BF16))
        onesc = ctx.enter_context(nc.sbuf_tensor("onesc", [128, 1], BF16))
        scratch4 = ctx.enter_context(nc.sbuf_tensor("scratch4", [128, TPC], F32))
        sil = scratch4[:, 0:512]
        xr_sb = scratch4[:, 512:1024]
        o_sb = scratch4[:, 0:512]
        pool = [ctx.enter_context(nc.psum_tensor(f"pp{i}", [128, 512], F32)) for i in range(8)]

        s_cast = Sem(nc, ctx, "s_cast")
        s_tr = Sem(nc, ctx, "s_tr")
        s_st = Sem(nc, ctx, "s_st")
        s_ln = Sem(nc, ctx, "s_ln")
        s_w = Sem(nc, ctx, "s_w")
        s_g = Sem(nc, ctx, "s_g")
        s_u = Sem(nc, ctx, "s_u")
        s_gu = Sem(nc, ctx, "s_gu")
        s_wd = Sem(nc, ctx, "s_wd")
        s_dn = Sem(nc, ctx, "s_dn")
        s_x = Sem(nc, ctx, "s_x")
        s_o = Sem(nc, ctx, "s_o")
        s_out = Sem(nc, ctx, "s_out")
        sems = {"cast": s_cast, "tr": s_tr, "st": s_st}
        block = ctx.enter_context(nc.Block())

        @block.gpsimd
        def _(g):
            s_cast.inc(g.memset(ones_r[:, :], 1.0))
            s_cast.inc(g.memset(onesc[:, :], 1.0))

        staged = emit_weight_stage(nc, block, ctx, sems, [
            (w_gate, True, False), (w_up, True, False), (w_down, True, False)])
        wgT, wuT, wdT = staged[0][1], staged[1][1], staged[2][1]
        stage_done = s_st.n

        emit_ln_fm(nc, block, ctx, pool, {"onesc": onesc, "ones_r": ones_r},
                   x_in, ln_sb, sems, LN_EPS, s_ln, xc=scratch4[:, :])
        P_mr, P_vr, P_mb, P_rb = emit_ln_fm.last_mr

        gu_mm_done = {}
        for hh in range(0 if ln_only else HC):
            buf = hh % 2

            @block.gpsimd
            def _(g, hh=hh, buf=buf):
                if hh >= 2:
                    s_g.wait(g, gu_mm_done[hh - 2])
                else:
                    g.wait_ge(s_st.h, stage_done)
                for (wT, dst) in ((wgT, wg_sb[buf]), (wuT, wu_sb[buf])):
                    for q in range(4):
                        src = wT.rearrange("(kc p) h -> p kc h", p=128)[
                            :, q * 4:(q + 1) * 4, hh * 128:(hh + 1) * 128]
                        dstv = dst[:, :].rearrange("p (kc o) -> p kc o", kc=KC)[
                            :, q * 4:(q + 1) * 4, :]
                        inst = g.dma_start(out=dstv, in_=src)
                        s_w.inc(inst, 16)

            for tb in range(2):
                @block.tensor
                def _(t, hh=hh, buf=buf, tb=tb):
                    s_w.wait(t)
                    if hh == 0 and tb == 0:
                        s_ln.wait(t)
                    s_gu.wait(t)
                    for k in range(KC):
                        t.matmul(pool[5][:, :], wg_sb[buf][:, k * 128:(k + 1) * 128],
                                 ln_sb[:, k * TPC + tb * 512: k * TPC + tb * 512 + 512],
                                 start=(k == 0), stop=(k == KC - 1))
                    for k in range(KC):
                        inst = t.matmul(pool[6][:, :], wu_sb[buf][:, k * 128:(k + 1) * 128],
                                        ln_sb[:, k * TPC + tb * 512: k * TPC + tb * 512 + 512],
                                        start=(k == 0), stop=(k == KC - 1))
                    s_g.inc(inst)
                    if tb == 1:
                        gu_mm_done[hh] = s_g.n

                @block.scalar
                def _(s, hh=hh, tb=tb):
                    s_g.wait(s)
                    s_gu.wait(s)
                    inst = s.activation(sil, pool[5][:, :], AF.Silu)
                    s_u.inc(inst)

                @block.vector
                def _(v, hh=hh, tb=tb):
                    s_u.wait(v)
                    inst = v.tensor_mul(gu[:, hh * TPC + tb * 512: hh * TPC + tb * 512 + 512],
                                        pool[6][:, :], sil)
                    s_gu.inc(inst)
        gu_done = s_gu.n

        for tb in range(0 if (ln_only or no_down) else 2):
            for dh in range(2):
                dn_mm_done = {}
                grp_start = s_dn.n
                for hh in range(HC):
                    buf = hh % 2

                    @block.gpsimd
                    def _(g, tb=tb, dh=dh, hh=hh, buf=buf, grp_start=grp_start):
                        if hh >= 2:
                            s_dn.wait(g, dn_mm_done[hh - 2])
                        else:
                            s_dn.wait(g, grp_start)
                        inst = g.dma_start(out=wd_sb[buf][:, :],
                                           in_=wdT[hh * 128:(hh + 1) * 128,
                                                   dh * 1024:(dh + 1) * 1024])
                        s_wd.inc(inst, 16)

                    @block.tensor
                    def _(t, tb=tb, dh=dh, hh=hh, buf=buf):
                        s_wd.wait(t)
                        if hh == 0:
                            s_gu.wait(t, gu_done)
                            s_o.wait(t)
                        for d in range(8):
                            inst = t.matmul(pool[d][:, :], wd_sb[buf][:, d * 128:(d + 1) * 128],
                                            gu[:, hh * TPC + tb * 512: hh * TPC + tb * 512 + 512],
                                            start=(hh == 0), stop=(hh == HC - 1))
                        s_dn.inc(inst)
                        dn_mm_done[hh] = s_dn.n

                for d in range(8):
                    @block.sync
                    def _(s, tb=tb, dh=dh, d=d):
                        s_gu.wait(s, gu_done)
                        s_o.wait(s)
                        inst = s.dma_start(out=xr_sb,
                                           in_=x_in[(dh * 8 + d) * 128:(dh * 8 + d + 1) * 128,
                                                    tb * 512:(tb + 1) * 512])
                        s_x.inc(inst, 16)

                    @block.vector
                    def _(v, tb=tb, dh=dh, d=d):
                        s_dn.wait(v)
                        s_x.wait(v)
                        s_out.wait(v)
                        inst = v.tensor_add(o_sb, pool[d][:, :], xr_sb)
                        s_o.inc(inst)

                    @block.sync
                    def _(s, tb=tb, dh=dh, d=d):
                        s_o.wait(s)
                        inst = s.dma_start(
                            out=x2_fm[(dh * 8 + d) * 128:(dh * 8 + d + 1) * 128,
                                      tb * 512:(tb + 1) * 512],
                            in_=o_sb)
                        s_out.inc(inst, 16)

        if debug:
            st2_dbg = nc.dram_tensor("st2_dbg", [2, TPC], F32, kind="ExternalOutput").ap()
            bc_dbg = nc.dram_tensor("bc_dbg", [2 * 128, TPC], BF16, kind="ExternalOutput").ap()
            @block.sync
            def _(s):
                s_out.wait(s)
                inst = s.dma_start(out=st2_dbg[0:1, :], in_=P_mr[:, :])
                s_out.inc(inst, 16)
                inst = s.dma_start(out=st2_dbg[1:2, :], in_=P_vr[:, :])
                s_out.inc(inst, 16)
                inst = s.dma_start(out=bc_dbg[0:128, :], in_=P_mb[:, :])
                s_out.inc(inst, 16)
                inst = s.dma_start(out=bc_dbg[128:256, :], in_=P_rb[:, :])
                s_out.inc(inst, 16)
            ln_dbg = nc.dram_tensor("ln_dbg", [128, KC * TPC], BF16, kind="ExternalOutput").ap()
            gu_dbg = nc.dram_tensor("gu_dbg", [128, HC * TPC], BF16, kind="ExternalOutput").ap()
            @block.sync
            def _(s):
                s_out.wait(s)
                inst = s.dma_start(out=ln_dbg, in_=ln_sb[:, :])
                s_out.inc(inst, 16)
                inst = s.dma_start(out=gu_dbg, in_=gu[:, :])
                s_out.inc(inst, 16)

        @block.sync
        def _(s):
            s_out.wait(s)

    return nc



DIN = 4096
MNH = 64
MHD = 64
DST = 128
DCONV = 4
DXBC = DIN + 2 * DST
DINPROJ = 2 * DIN + 2 * DST + MNH
XCC = DXBC // 128
MCH = 8
RMS_EPS = 1e-5


def build_D(debug=False):
    nc = bass.Bass("TRN2", target_bir_lowering=False)
    from concourse.bass import ds
    x_in = nc.dram_tensor("x2_fm", [D, TPC], F32, kind="ExternalInput").ap()
    w_in = nc.dram_tensor("w_in", [D, DINPROJ], F32, kind="ExternalInput").ap()
    conv_w = nc.dram_tensor("conv_w", [DXBC, DCONV], F32, kind="ExternalInput").ap()
    conv_b = nc.dram_tensor("conv_b", [DXBC], F32, kind="ExternalInput").ap()
    dt_bias = nc.dram_tensor("dt_bias", [MNH], F32, kind="ExternalInput").ap()
    a_log = nc.dram_tensor("a_log", [MNH], F32, kind="ExternalInput").ap()
    d_tiled = nc.dram_tensor("d_tiled", [128, 32], F32, kind="ExternalInput").ap()
    norm_w = nc.dram_tensor("norm_w", [DIN], F32, kind="ExternalInput").ap()
    w_out = nc.dram_tensor("w_out", [DIN, D], F32, kind="ExternalInput").ap()
    cmask = nc.dram_tensor("cmask", [2, 64, 16], F32, kind="ExternalInput").ap()
    cmask2 = nc.dram_tensor("cmask2", [64, 256], F32, kind="ExternalInput").ap()
    tscale = nc.dram_tensor("tscale", [128, 2], F32, kind="ExternalInput").ap()
    x3_fm = nc.dram_tensor("x3_fm", [D, TPC], F32, kind="ExternalOutput").ap()

    dkind = "ExternalOutput" if debug else "Internal"
    z_dram = nc.dram_tensor("z_dram", [MCH, 128, 32, 128], BF16, kind=dkind).ap()
    xact_dram = nc.dram_tensor("xact_dram", [MCH, 128, 32, 128], BF16, kind=dkind).ap()
    states_dram = nc.dram_tensor("states_dram", [MCH, 128, DIN], BF16, kind=dkind).ap()
    tails_c = nc.dram_tensor("tails_c", [2, XCC, 128, 3], BF16, kind="Internal").ap()
    tails_g = nc.dram_tensor("tails_g", [20, XCC, 128, 3], BF16, kind="Internal").ap()
    summ_c = nc.dram_tensor("summ_c", [2, 128, DIN], BF16, kind="Internal").ap()
    summ_g = nc.dram_tensor("summ_g", [NC * 2, 128, DIN], BF16, kind="Internal").ap()
    la_c = nc.dram_tensor("la_c", [2, MNH], F32, kind="Internal").ap()
    la_g = nc.dram_tensor("la_g", [NC * 2, MNH], F32, kind="Internal").ap()
    wt_dram = nc.dram_tensor("wt_dram", [2, 16, MNH], F32, kind="Internal").ap()

    with ExitStack() as ctx:
        identb = ctx.enter_context(nc.sbuf_tensor("identb", [128, 128], BF16))
        identf = ctx.enter_context(nc.sbuf_tensor("identf", [128, 128], F32))
        ones_r = ctx.enter_context(nc.sbuf_tensor("ones_r", [1, 512], BF16))
        onesf = ctx.enter_context(nc.sbuf_tensor("onesf", [1, 128], F32))
        onesb512 = ctx.enter_context(nc.sbuf_tensor("onesb512", [128, 512], BF16))
        onesc = ctx.enter_context(nc.sbuf_tensor("onesc", [128, 1], BF16))
        ones64 = ctx.enter_context(nc.sbuf_tensor("ones64", [64, 1], F32))
        segmask = ctx.enter_context(nc.sbuf_tensor("segmask", [128, 128], BF16))
        eye64 = ctx.enter_context(nc.sbuf_tensor("eye64", [64, 64], BF16))
        cw_sb = ctx.enter_context(nc.sbuf_tensor("cw_sb", [128, XCC * DCONV], F32))
        cb_sb = ctx.enter_context(nc.sbuf_tensor("cb_sb", [128, XCC], F32))
        dtb_sb = ctx.enter_context(nc.sbuf_tensor("dtb_sb", [64, 1], F32))
        nA_sb = ctx.enter_context(nc.sbuf_tensor("nA_sb", [64, 1], F32))
        dtil_sb = ctx.enter_context(nc.sbuf_tensor("dtil_sb", [128, 32], F32))
        nw_sb = ctx.enter_context(nc.sbuf_tensor("nw_sb", [128, 32], F32))
        cmask_sb = ctx.enter_context(nc.sbuf_tensor("cmask_sb", [64, 2 * 16], F32))
        tsc_sb = ctx.enter_context(nc.sbuf_tensor("tsc_sb", [128, 2], F32))
        dt_sb = ctx.enter_context(nc.sbuf_tensor("dt_sb", [64, TPC], F32))
        dtf_sb = ctx.enter_context(nc.sbuf_tensor("dtf_sb", [64, TPC], F32))
        lndtf_sb = ctx.enter_context(nc.sbuf_tensor("lndtf_sb", [64, TPC], F32))
        cum_sb = ctx.enter_context(nc.sbuf_tensor("cum_sb", [64, TPC], F32))
        cl_sb = ctx.enter_context(nc.sbuf_tensor("cl_sb", [64, MCH], F32))
        bc_sb = ctx.enter_context(nc.sbuf_tensor("bc_sb", [128, 2 * TPC], BF16))
        scrD = ctx.enter_context(nc.sbuf_tensor("scrD", [128, TPC], F32))
        scr64 = ctx.enter_context(nc.sbuf_tensor("scr64", [64, 256], F32))
        pd = [ctx.enter_context(nc.psum_tensor(f"pd{i}", [128, 512], F32)) for i in range(6)]
        pb = [ctx.enter_context(nc.psum_tensor(f"pb{i}", [128, 512], BF16)) for i in range(2)]

        s_pre = Sem(nc, ctx, "s_pre")
        s_ln = Sem(nc, ctx, "s_ln")
        block = ctx.enter_context(nc.Block())

        @block.gpsimd
        def _(g):
            make_identity(nc, identb[:, :], s_pre)
            make_identity(nc, identf[:, :], s_pre)
            s_pre.inc(g.memset(ones_r[:, :], 1.0))
            s_pre.inc(g.memset(onesf[:, :], 1.0))
            s_pre.inc(g.memset(onesb512[:, :], 1.0))
            s_pre.inc(g.memset(onesc[:, :], 1.0))
            s_pre.inc(g.memset(ones64[:, :], 1.0))
            s_pre.inc(g.memset(segmask[:, :], 0.0))
            s_pre.wait(g)
            inst = g.affine_select(out=segmask[:, :], in_=segmask[:, :],
                                   compare_op=OP.is_ge, fill=NEG_BIG,
                                   base=0, channel_multiplier=-1, pattern=[[1, 128]])
            s_pre.inc(inst)
            make_identity(nc, eye64[:, :], s_pre)
            s_pre.inc(g.dma_start(out=cw_sb[:, :].rearrange("p (c k) -> p c k", c=XCC),
                                  in_=conv_w.rearrange("(c p) k -> p c k", p=128)), 16)
            s_pre.inc(g.dma_start(out=cb_sb[:, :], in_=conv_b.rearrange("(c p) -> p c", p=128), allow_slow_non_contiguous=True), 16)
            s_pre.inc(g.dma_start(out=dtb_sb[:, :], in_=dt_bias.rearrange("(p a) -> p a", a=1)), 16)
            s_pre.inc(g.dma_start(out=nA_sb[:, :], in_=a_log.rearrange("(p a) -> p a", a=1)), 16)
            s_pre.inc(g.dma_start(out=dtil_sb[:, :], in_=d_tiled), 16)
            s_pre.inc(g.dma_start(out=nw_sb[:, :], in_=norm_w.rearrange("(c p) -> p c", p=128), allow_slow_non_contiguous=True), 16)
            s_pre.inc(g.dma_start(out=cmask_sb[:, :].rearrange("p (b m) -> p b m", b=2),
                                  in_=cmask.rearrange("b p m -> p b m")), 16)
            s_pre.inc(g.dma_start(out=tsc_sb[:, :], in_=tscale), 16)

        @block.scalar
        def _(s):
            s_pre.wait(s)
            inst = s.activation(nA_sb[:, :], nA_sb[:, :], AF.Exp)
            s_pre.inc(inst)

        sA = ExitStack()
        ln2_sb = sA.enter_context(nc.sbuf_tensor("ln2_sb", [128, KC * TPC], BF16))
        xbc = sA.enter_context(nc.sbuf_tensor("xbc", [128, XCC * TPC], BF16))
        wi_sb = [sA.enter_context(nc.sbuf_tensor(f"wi_sb{i}", [128, KC * 128], BF16)) for i in range(2)]
        ev_sb = sA.enter_context(nc.sbuf_tensor("ev_sbD", [128, TPC], BF16))

        emit_ln_fm(nc, block, sA, pd, {"onesc": onesc, "ones_r": ones_r},
                   x_in, ln2_sb, {}, LN_EPS, s_ln, tag="D", xc=scrD[:, :], sem_ctx=ctx)

        s_w = Sem(nc, ctx, "s_w")
        s_mm = Sem(nc, ctx, "s_mm")
        s_ev = Sem(nc, ctx, "s_ev")
        s_st = Sem(nc, ctx, "s_st")
        OCIN = DINPROJ // 128 + 1
        ip_done = {}
        for o in range(OCIN):
            buf = o % 2
            orows = 64 if o == OCIN - 1 else 128

            @block.gpsimd
            def _(g, o=o, buf=buf, orows=orows):
                if o >= 2:
                    s_mm.wait(g, ip_done[o - 2])
                for q in range(4):
                    src = w_in.rearrange("(kc p) o -> p kc o", p=128)[
                        :, q * 4:(q + 1) * 4, o * 128:o * 128 + orows]
                    dst = wi_sb[buf][:, :].rearrange("p (kc o) -> p kc o", kc=KC)[
                        :, q * 4:(q + 1) * 4, :orows]
                    inst = g.dma_start(out=dst, in_=src)
                    s_w.inc(inst, 16)

            for tb in range(2):
                @block.tensor
                def _(t, o=o, buf=buf, tb=tb, orows=orows):
                    s_w.wait(t)
                    if o == 0 and tb == 0:
                        s_ln.wait(t)
                    s_ev.wait(t)
                    for k in range(KC):
                        inst = t.matmul(pd[4 + tb][:orows, :],
                                        wi_sb[buf][:, k * 128:k * 128 + orows],
                                        ln2_sb[:, k * TPC + tb * 512: k * TPC + tb * 512 + 512],
                                        start=(k == 0), stop=(k == KC - 1))
                    s_mm.inc(inst)
                    if tb == 1:
                        ip_done[o] = s_mm.n

                @block.scalar
                def _(s, o=o, tb=tb, orows=orows):
                    s_mm.wait(s)
                    s_st.wait(s)
                    if o < 32 or (32 <= o < 66):
                        inst = s.activation(ev_sb[:orows, tb * 512:(tb + 1) * 512],
                                            pd[4 + tb][:orows, :], AF.Identity)
                        s_ev.inc(inst)
                    else:
                        inst = s.activation(dt_sb[:, tb * 512:(tb + 1) * 512],
                                            pd[4 + tb][:64, :], AF.Identity)
                        s_ev.inc(inst)

            if o < 32:
                @block.sync
                def _(s, o=o):
                    s_ev.wait(s)
                    zv = z_dram.rearrange("c p ch t -> p c ch t")
                    for c in range(MCH):
                        inst = s.dma_start(
                            out=zv[:, c:c + 1, o:o + 1, :],
                            in_=ev_sb[:, c * 128:(c + 1) * 128]
                                .rearrange("p (a b t) -> p a b t", a=1, b=1))
                        s_st.inc(inst, 16)
            elif o < 66:
                @block.vector
                def _(v, o=o):
                    s_ev.wait(v)
                    inst = v.tensor_copy(xbc[:, (o - 32) * TPC:(o - 31) * TPC], ev_sb[:, :])
                    s_st.inc(inst)

        s_tl = Sem(nc, ctx, "s_tl")
        tail_sb = sA.enter_context(nc.sbuf_tensor("tail_sb", [128, 2 * XCC * 3], BF16))

        @block.gpsimd
        def _(g):
            s_st.wait(g)
            s_ev.wait(g)
            for b in range(2):
                inst = g.dma_start(
                    out=tails_c.rearrange("b c p k -> p (b c) k")[:, b * XCC:(b + 1) * XCC, :],
                    in_=xbc[:, :].rearrange("p (c t) -> p c t", c=XCC)[
                        :, :, b * 512 + 509: b * 512 + 512])
                s_tl.inc(inst, 16)
            inst = g.memset(tail_sb[:, :], 0.0)
            s_tl.inc(inst)
            g.wait_ge(s_tl.h, s_tl.n)
            for pad in range(2):
                inst = g.dma_start(
                    out=tails_g.rearrange("s c p k -> p s c k")[:, pad:pad + 1, :, :],
                    in_=tail_sb[:, 0:XCC * 3].rearrange("p (a c k) -> p a c k", a=1, c=XCC))
                s_tl.inc(inst, 16)
            g.wait_ge(s_tl.h, s_tl.n)
            inst = g.collective_compute(
                "AllGather", OP.bypass, replica_groups=[list(range(NC))],
                ins=[tails_c.opt()], outs=[tails_g[2:18, :, :, :].opt()])
            s_tl.inc(inst)
            g.wait_ge(s_tl.h, s_tl.n)
            inst = g.dma_start(
                out=tails_g.rearrange("s c p k -> p s c k")[:, 19:20, :, :],
                in_=tails_g.rearrange("s c p k -> p s c k")[:, 16:17, :, :])
            s_tl.inc(inst, 16)
            g.wait_ge(s_tl.h, s_tl.n)
            pid = g.snap(g.partition_id(), min_val=0, max_val=NC - 1)
            for b, ivx in ((0, pid * 2), (1, pid * 2 + 5)):
                src = tails_g.rearrange("s c p k -> p s c k")[:, ds(ivx, 1), :, :] \
                             .rearrange("p s c k -> p (s c) k")
                inst = g.dma_start(
                    out=tail_sb[:, b * XCC * 3:(b + 1) * XCC * 3]
                        .rearrange("p (c k) -> p c k", c=XCC),
                    in_=src)
                s_tl.inc(inst, 16)

        @block.vector
        def _(v):
            s_tl.wait(v)
            for b in range(2):
                inst = v.tensor_scalar_mul(
                    tail_sb[:, b * XCC * 3:(b + 1) * XCC * 3],
                    tail_sb[:, b * XCC * 3:(b + 1) * XCC * 3],
                    tsc_sb[:, b:b + 1])
                s_tl.inc(inst)

        s_cd = Sem(nc, ctx, "s_cd")
        s_cm = Sem(nc, ctx, "s_cm")
        s_cv = Sem(nc, ctx, "s_cv")
        diag_sb = sA.enter_context(nc.sbuf_tensor("diag_sb", [128, 5 * 128], BF16))
        ext_sb = sA.enter_context(nc.sbuf_tensor("ext_sb", [128, 520], BF16))

        for cch in range(XCC):
            @block.vector
            def _(v, cch=cch):
                s_pre.wait(v)
                s_cm.wait(v)
                for k in range(DCONV):
                    inst = v.tensor_scalar_mul(diag_sb[:, k * 128:(k + 1) * 128],
                                               identb[:, :], cw_sb[:, cch * DCONV + k: cch * DCONV + k + 1])
                    s_cd.inc(inst)
                inst = v.tensor_scalar_mul(diag_sb[:, 4 * 128:5 * 128], identb[:, :],
                                           cb_sb[:, cch:cch + 1])
                s_cd.inc(inst)

            for tb in range(2):
                @block.vector
                def _(v, cch=cch, tb=tb):
                    s_tl.wait(v)
                    s_cm.wait(v)
                    toff = tb * XCC * 3 + cch * 3
                    inst = v.tensor_copy(ext_sb[:, 0:3], tail_sb[:, toff:toff + 3])
                    s_cv.inc(inst)
                    inst = v.tensor_copy(ext_sb[:, 3:515],
                                         xbc[:, cch * TPC + tb * 512: cch * TPC + tb * 512 + 512])
                    s_cv.inc(inst)

                @block.tensor
                def _(t, cch=cch, tb=tb):
                    s_cd.wait(t)
                    s_cv.wait(t)
                    s_cm.wait(t)
                    for k in range(DCONV):
                        t.matmul(pd[2][:, :], diag_sb[:, k * 128:(k + 1) * 128],
                                 ext_sb[:, k:k + 512], start=(k == 0), stop=False)
                    inst = t.matmul(pd[2][:, :], diag_sb[:, 4 * 128:5 * 128],
                                    onesb512[:, :], start=False, stop=True)
                    s_cm.inc(inst)

                @block.scalar
                def _(s, cch=cch, tb=tb):
                    s_cm.wait(s)
                    inst = s.activation(xbc[:, cch * TPC + tb * 512: cch * TPC + tb * 512 + 512],
                                        pd[2][:, :], AF.Silu)
                    s_cm.inc(inst)

            if cch < 32:
                @block.sync
                def _(s, cch=cch):
                    s_cm.wait(s)
                    xv = xact_dram.rearrange("c p ch t -> p c ch t")
                    for c in range(MCH):
                        inst = s.dma_start(
                            out=xv[:, c:c + 1, cch:cch + 1, :],
                            in_=xbc[:, cch * TPC + c * 128: cch * TPC + (c + 1) * 128]
                                .rearrange("p (a b t) -> p a b t", a=1, b=1))
                        s_st.inc(inst, 16)

        @block.vector
        def _(v):
            s_cm.wait(v)
            inst = v.tensor_copy(bc_sb[:, 0:TPC], xbc[:, 32 * TPC:33 * TPC])
            s_cv.inc(inst)
            inst = v.tensor_copy(bc_sb[:, TPC:2 * TPC], xbc[:, 33 * TPC:34 * TPC])
            s_cv.inc(inst)
        bc_ready = s_cv.n
        xdram_ready = s_st.n

        sA.close()

        s_dt = Sem(nc, ctx, "s_dt")

        @block.vector
        def _(v):
            s_ev.wait(v)
            s_pre.wait(v)
            inst = v.tensor_scalar(out=dt_sb[:, :], in0=dt_sb[:, :], scalar1=dtb_sb[:, :],
                                   scalar2=None, op0=OP.add)
            s_dt.inc(inst)

        @block.scalar
        def _(s):
            s_dt.wait(s)
            inst = s.activation(scrD[:64, 0:TPC], dt_sb[:, :], AF.Abs)
            s_dt.inc(inst)
            s_dt.wait(s)
            inst = s.activation(scrD[:64, 0:TPC], scrD[:64, 0:TPC], AF.Exp, scale=-1.0)
            s_dt.inc(inst)
            s_dt.wait(s)
            inst = s.activation(scrD[:64, 0:TPC], scrD[:64, 0:TPC], AF.Ln, bias=ones64[:, :])
            s_dt.inc(inst)

        @block.vector
        def _(v):
            s_dt.wait(v)
            inst = v.tensor_scalar_max(dtf_sb[:, :], dt_sb[:, :], 0.0)
            s_dt.inc(inst)
            s_dt.wait(v)
            inst = v.tensor_add(dtf_sb[:, :], dtf_sb[:, :], scrD[:64, 0:TPC])
            s_dt.inc(inst)

        @block.scalar
        def _(s):
            s_dt.wait(s)
            inst = s.activation(lndtf_sb[:, :], dtf_sb[:, :], AF.Ln)
            s_dt.inc(inst)

        @block.vector
        def _(v):
            s_dt.wait(v)
            inst = v.tensor_scalar(out=cum_sb[:, :], in0=dtf_sb[:, :], scalar1=nA_sb[:, :],
                                   scalar2=-1.0, op0=OP.mult, op1=OP.mult)
            s_dt.inc(inst)
            for c in range(MCH):
                s_dt.wait(v)
                inst = v.tensor_reduce(cl_sb[:, c:c + 1], cum_sb[:, c * 128:(c + 1) * 128],
                                       axis=mybir.AxisListType.X, op=OP.add)
                s_dt.inc(inst)
                s_dt.wait(v)
                inst = v.tensor_tensor_scan(out=cum_sb[:, c * 128:(c + 1) * 128],
                                            data0=cum_sb[:, c * 128:(c + 1) * 128],
                                            data1=cum_sb[:, c * 128:(c + 1) * 128],
                                            initial=0.0, op0=OP.add, op1=OP.bypass)
                s_dt.inc(inst)
        dt_ready = s_dt.n

        cumrel_dram = nc.dram_tensor("cumrel_dram", [2, 64, 128], F32, kind="Internal").ap()
        t_sb = ctx.enter_context(nc.sbuf_tensor("t_sb", [128, 32 * TPC], BF16))
        x_sl = ctx.enter_context(nc.sbuf_tensor("x_sl", [128, 32 * 128], BF16))
        z_sl = ctx.enter_context(nc.sbuf_tensor("z_sl", [128, 32 * 128], BF16))
        xtm_sb = ctx.enter_context(nc.sbuf_tensor("xtm_sb", [128, 32 * 128], BF16))
        btm_sb = ctx.enter_context(nc.sbuf_tensor("btm_sb", [128, 128], BF16))
        wtm_sb = ctx.enter_context(nc.sbuf_tensor("wtm_sb", [128, 64], F32))
        xs_sb = ctx.enter_context(nc.sbuf_tensor("xs_sb", [128, 64], BF16))
        st_sb = ctx.enter_context(nc.sbuf_tensor("st_sb", [128, DIN], BF16))
        prevF = ctx.enter_context(nc.sbuf_tensor("prevF", [128, DIN], F32))
        prevS = prevF
        prev2 = prevF
        prev2b = ctx.enter_context(nc.sbuf_tensor("prev2b", [128, DIN], BF16))
        carr = [ctx.enter_context(nc.sbuf_tensor(f"carr{i}", [128, DIN], BF16)) for i in range(2)]
        aball = ctx.enter_context(nc.sbuf_tensor("aball", [128, MCH * 64], F32))
        arow_sb = ctx.enter_context(nc.sbuf_tensor("arow_sb", [1, 64], F32))
        clrow_sb = ctx.enter_context(nc.sbuf_tensor("clrow_sb", [1, 64], F32))
        w1_sb = ctx.enter_context(nc.sbuf_tensor("w1_sb", [64, 128], F32))

        s_x5 = Sem(nc, ctx, "s_x5")
        s_p5 = Sem(nc, ctx, "s_p5")
        s_t5 = Sem(nc, ctx, "s_t5")
        s_e5 = Sem(nc, ctx, "s_e5")
        s_m5 = Sem(nc, ctx, "s_m5")
        s_s5 = Sem(nc, ctx, "s_s5")

        def emit_xtm(c, s_xload, xload_n):
            for g4 in range(8):
                @block.tensor
                def _(t, g4=g4):
                    t.wait_ge(s_xload.h, xload_n)
                    s_e5.wait(t)
                    for j in range(4):
                        ch = g4 * 4 + j
                        inst = t.transpose(pb[0][:, j * 128:(j + 1) * 128],
                                           x_sl[:, ch * 128:(ch + 1) * 128], identb[:, :])
                    s_t5.inc(inst)

                @block.scalar
                def _(s, g4=g4):
                    s_t5.wait(s)
                    inst = s.activation(xtm_sb[:, g4 * 512:(g4 + 1) * 512], pb[0][:, :],
                                        AF.Identity)
                    s_e5.inc(inst)

        for c in range(MCH):
            blk, cb = c // 4, c % 4

            @block.gpsimd
            def _(g, c=c):
                if c == 0:
                    g.wait_ge(s_st.h, xdram_ready)
                s_t5.wait(g)
                inst = g.dma_start(
                    out=x_sl[:, :].rearrange("p (a q) -> p a q", a=1),
                    in_=xact_dram.rearrange("c p ch t -> p c (ch t)")[:, c:c + 1, :])
                s_x5.inc(inst, 16)

            @block.vector
            def _(v, c=c):
                v.wait_ge(s_dt.h, dt_ready)
                s_t5.wait(v)
                inst = v.tensor_scalar(out=w1_sb[:, :], in0=cum_sb[:, c * 128:(c + 1) * 128],
                                       scalar1=cl_sb[:, c:c + 1], scalar2=None, op0=OP.subtract)
                s_p5.inc(inst)

            @block.scalar
            def _(s, c=c):
                s_p5.wait(s)
                inst = s.activation(w1_sb[:, :], w1_sb[:, :], AF.Exp, scale=-1.0)
                s_p5.inc(inst)

            @block.vector
            def _(v, c=c):
                s_p5.wait(v)
                inst = v.tensor_mul(w1_sb[:, :], w1_sb[:, :], dtf_sb[:, c * 128:(c + 1) * 128])
                s_p5.inc(inst)

            @block.tensor
            def _(t, c=c):
                s_p5.wait(t)
                s_pre.wait(t)
                s_e5.wait(t)
                inst = t.transpose(pd[0][0:1, 0:64], cl_sb[:, c:c + 1], identf[0:64, 0:64])
                s_t5.inc(inst)
                inst = t.transpose(pd[1][:, 0:64], w1_sb[:, :], identf[0:64, 0:64])
                s_t5.inc(inst)

            @block.scalar
            def _(s, c=c):
                s_t5.wait(s)
                inst = s.activation(clrow_sb[:, :], pd[0][0:1, 0:64], AF.Identity)
                s_e5.inc(inst)
                inst = s.activation(wtm_sb[:, :], pd[1][:, 0:64], AF.Identity)
                s_e5.inc(inst)
                s_e5.wait(s)
                inst = s.activation(arow_sb[:, :], clrow_sb[:, :], AF.Exp)
                s_e5.inc(inst)

            @block.tensor
            def _(t, c=c):
                s_e5.wait(t)
                inst = t.matmul(pd[2][:, 0:64], onesf[0:1, 0:128], arow_sb[:, :],
                                start=True, stop=True)
                s_t5.inc(inst)

            @block.scalar
            def _(s, c=c):
                s_t5.wait(s)
                inst = s.activation(aball[:, c * 64:(c + 1) * 64], pd[2][:, 0:64], AF.Identity)
                s_e5.inc(inst)

            emit_xtm(c, s_x5, s_x5.n)

            @block.tensor
            def _(t, c=c):
                v = bc_ready
                t.wait_ge(s_cv.h, v)
                s_e5.wait(t)
                inst = t.transpose(pb[1][:, 0:128], bc_sb[:, c * 128:(c + 1) * 128], identb[:, :])
                s_t5.inc(inst)

            @block.scalar
            def _(s, c=c):
                s_t5.wait(s)
                inst = s.activation(btm_sb[:, :], pb[1][:, 0:128], AF.Identity)
                s_e5.inc(inst)

            for h in range(MNH):
                @block.scalar
                def _(s, c=c, h=h):
                    s_e5.wait(s)
                    s_m5.wait(s)
                    xoff = (h // 2) * 128 + (h % 2) * 64
                    inst = s.activation(xs_sb[:, 0:64], xtm_sb[:, xoff:xoff + 64],
                                        AF.Identity, scale=wtm_sb[:, h:h + 1])
                    s_p5.inc(inst)

                @block.tensor
                def _(t, c=c, h=h):
                    s_p5.wait(t)
                    s_e5.wait(t)
                    inst = t.matmul(pd[3][:, 0:64], btm_sb[:, :], xs_sb[:, 0:64],
                                    start=True, stop=True)
                    s_m5.inc(inst)

                @block.scalar
                def _(s, c=c, h=h):
                    s_m5.wait(s)
                    inst = s.activation(st_sb[:, h * 64:(h + 1) * 64], pd[3][:, 0:64],
                                        AF.Identity)
                    s_e5.inc(inst)

            @block.gpsimd
            def _(g, c=c):
                s_e5.wait(g)
                inst = g.dma_start(out=states_dram.rearrange("c p q -> p c q")[:, c:c + 1, :],
                                   in_=st_sb[:, :].rearrange("p (a q) -> p a q", a=1))
                s_s5.inc(inst, 16)

            @block.vector
            def _(v, c=c, cb=cb, blk=blk):
                s_e5.wait(v)
                if cb == 0:
                    inst = v.tensor_copy(prevS[:, :], st_sb[:, :])
                    s_s5.inc(inst)
                else:
                    for h in range(MNH):
                        s_s5.wait(v)
                        inst = v.scalar_tensor_tensor(
                            out=prevS[:, h * 64:(h + 1) * 64],
                            in0=prevS[:, h * 64:(h + 1) * 64],
                            scalar=aball[:, c * 64 + h: c * 64 + h + 1],
                            in1=st_sb[:, h * 64:(h + 1) * 64],
                            op0=OP.mult, op1=OP.add)
                        s_s5.inc(inst)
                if cb == 3:
                    s_s5.wait(v)
                    inst = v.tensor_add(scrD[:64, 0:1], cl_sb[:, c - 3:c - 2], cl_sb[:, c - 2:c - 1])
                    s_s5.inc(inst)
                    s_s5.wait(v)
                    inst = v.tensor_add(scrD[:64, 0:1], scrD[:64, 0:1], cl_sb[:, c - 1:c])
                    s_s5.inc(inst)
                    s_s5.wait(v)
                    inst = v.tensor_add(scrD[:64, 0:1], scrD[:64, 0:1], cl_sb[:, c:c + 1])
                    s_s5.inc(inst)

            if cb == 3:
                @block.gpsimd
                def _(g, blk=blk):
                    s_s5.wait(g)
                    inst = g.dma_start(out=summ_c.rearrange("b p q -> p b q")[:, blk:blk + 1, :],
                                       in_=prevS[:, :].rearrange("p (a q) -> p a q", a=1))
                    s_s5.inc(inst, 16)
                    inst = g.dma_start(out=la_c[blk:blk + 1, :].rearrange("a q -> q a"),
                                       in_=scrD[:64, 0:1],
                                       allow_slow_non_contiguous=True)
                    s_s5.inc(inst, 16)

        @block.gpsimd
        def _(g):
            s_s5.wait(g)
            inst = g.collective_compute(
                "AllGather", OP.bypass, replica_groups=[list(range(NC))],
                ins=[summ_c.opt()], outs=[summ_g.opt()])
            s_s5.inc(inst)
            inst = g.collective_compute(
                "AllGather", OP.bypass, replica_groups=[list(range(NC))],
                ins=[la_c.opt()], outs=[la_g.opt()])
            s_s5.inc(inst)

        la_sb = ctx.enter_context(nc.sbuf_tensor("la_sb", [64, 16], F32))
        prefD_sb = ctx.enter_context(nc.sbuf_tensor("prefD_sb", [64, 16], F32))
        cm2_sb = ctx.enter_context(nc.sbuf_tensor("cm2_sb", [64, 256], F32))
        wblk_sb = ctx.enter_context(nc.sbuf_tensor("wblk_sb", [64, 16], F32))
        wrow_sb = ctx.enter_context(nc.sbuf_tensor("wrow_sb", [1, 64], F32))
        wb_sb = ctx.enter_context(nc.sbuf_tensor("wb_sb", [128, 64], F32))
        sm_sb = st_sb
        s_c7 = Sem(nc, ctx, "s_c7")

        @block.gpsimd
        def _(g):
            s_s5.wait(g)
            inst = g.dma_start(out=la_sb[:, :],
                               in_=la_g.rearrange("m h -> h m"),
                               allow_slow_non_contiguous=True)
            s_c7.inc(inst, 16)
            inst = g.dma_start(out=cm2_sb[:, :], in_=cmask2)
            s_c7.inc(inst, 16)

        @block.vector
        def _(v):
            s_c7.wait(v)
            for m in range(16):
                inst = v.tensor_mul(wblk_sb[:, :], la_sb[:, :],
                                    cm2_sb[:, m * 16:(m + 1) * 16])
                s_c7.inc(inst)
                s_c7.wait(v)
                inst = v.tensor_reduce(prefD_sb[:, m:m + 1], wblk_sb[:, :],
                                       axis=mybir.AxisListType.X, op=OP.add)
                s_c7.inc(inst)
                s_c7.wait(v)

        for blk in range(2):
            @block.vector
            def _(v, blk=blk):
                s_c7.wait(v)
                s_pre.wait(v)
                inst = v.tensor_mul(wblk_sb[:, :], la_sb[:, :],
                                    cmask_sb[:, blk * 16:(blk + 1) * 16])
                s_c7.inc(inst)
                s_c7.wait(v)
                inst = v.tensor_reduce(scrD[:64, 1:2], wblk_sb[:, :],
                                       axis=mybir.AxisListType.X, op=OP.add)
                s_c7.inc(inst)
                s_c7.wait(v)
                inst = v.tensor_scalar(out=wblk_sb[:, :], in0=prefD_sb[:, :],
                                       scalar1=scrD[:64, 1:2], scalar2=None, op0=OP.subtract)
                s_c7.inc(inst)
                s_c7.wait(v)
                inst = v.tensor_scalar_max(wblk_sb[:, :], wblk_sb[:, :], 0.0)
                s_c7.inc(inst)

            @block.scalar
            def _(s, blk=blk):
                s_c7.wait(s)
                inst = s.activation(wblk_sb[:, :], wblk_sb[:, :], AF.Exp, scale=-1.0)
                s_c7.inc(inst)

            @block.vector
            def _(v, blk=blk):
                s_c7.wait(v)
                inst = v.tensor_mul(wblk_sb[:, :], wblk_sb[:, :],
                                    cmask_sb[:, blk * 16:(blk + 1) * 16])
                s_c7.inc(inst)

            @block.tensor
            def _(t, blk=blk):
                s_c7.wait(t)
                s_e5.wait(t)
                inst = t.transpose(pd[0][0:16, 0:64], wblk_sb[:, :], identf[0:64, 0:64])
                s_c7.inc(inst)

            @block.scalar
            def _(s, blk=blk):
                s_c7.wait(s)
                inst = s.activation(scrD[:16, 64:128], pd[0][0:16, 0:64], AF.Identity)
                s_c7.inc(inst)

            @block.gpsimd
            def _(g, blk=blk):
                s_c7.wait(g)
                wtv = wt_dram.rearrange("b m h -> (b m) h")
                inst = g.dma_start(out=wtv[blk * 16:(blk + 1) * 16, :], in_=scrD[:16, 64:128])
                s_c7.inc(inst, 16)

            for m in range(16):
                slab = m

                @block.gpsimd
                def _(g, blk=blk, m=m, slab=slab):
                    s_c7.wait(g)
                    wtv = wt_dram.rearrange("b m h -> (b m) h")
                    inst = g.dma_start(out=wrow_sb[:, :], in_=wtv[blk * 16 + m: blk * 16 + m + 1, :])
                    s_c7.inc(inst, 16)
                    s_m5.wait(g)
                    s_s5.wait(g)
                    inst = g.dma_start(out=sm_sb[:, :].rearrange("p (a q) -> p a q", a=1),
                                       in_=summ_g.rearrange("s p q -> p s q")[:, slab:slab + 1, :])
                    s_c7.inc(inst, 16)

                @block.tensor
                def _(t, blk=blk, m=m):
                    s_c7.wait(t)
                    s_e5.wait(t)
                    inst = t.matmul(pd[1][:, 0:64], onesf[0:1, 0:128], wrow_sb[:, :],
                                    start=True, stop=True)
                    s_c7.inc(inst)

                @block.scalar
                def _(s, blk=blk, m=m):
                    s_c7.wait(s)
                    inst = s.activation(wb_sb[:, :], pd[1][:, 0:64], AF.Identity)
                    s_c7.inc(inst)

                @block.vector
                def _(v, blk=blk, m=m):
                    s_c7.wait(v)
                    for h in range(MNH):
                        if m == 0:
                            inst = v.tensor_scalar(out=carr[blk][:, h * 64:(h + 1) * 64],
                                                   in0=sm_sb[:, h * 64:(h + 1) * 64],
                                                   scalar1=wb_sb[:, h:h + 1], scalar2=None,
                                                   op0=OP.mult)
                        else:
                            inst = v.scalar_tensor_tensor(
                                out=carr[blk][:, h * 64:(h + 1) * 64],
                                in0=sm_sb[:, h * 64:(h + 1) * 64],
                                scalar=wb_sb[:, h:h + 1],
                                in1=carr[blk][:, h * 64:(h + 1) * 64],
                                op0=OP.mult, op1=OP.add)
                        if h == MNH - 1:
                            s_s5.inc(inst)
                        else:
                            s_s5.inc(inst)
                    s_s5.wait(v)
                    s_c7.inc(v.engine_nop())
        carry_done = s_c7.n

        cumR = ctx.enter_context(nc.sbuf_tensor("cumR", [1, 8 * 128], F32))
        ncumR = ctx.enter_context(nc.sbuf_tensor("ncumR", [1, 8 * 128], F32))
        ldR = ctx.enter_context(nc.sbuf_tensor("ldR", [1, 8 * 128], F32))
        er_sb = ctx.enter_context(nc.sbuf_tensor("er_sb", [1, 128], F32))
        exb_sb = ctx.enter_context(nc.sbuf_tensor("exb_sb", [64, 128], F32))
        pdec_sb = ctx.enter_context(nc.sbuf_tensor("pdec_sb", [128, 128], BF16))
        p_sb = ctx.enter_context(nc.sbuf_tensor("p_sbD", [128, 128], BF16))
        yoff_sb = ctx.enter_context(nc.sbuf_tensor("yoff_sb", [64, 128], F32))
        y_c = ctx.enter_context(nc.sbuf_tensor("y_c", [128, 32 * 128], F32))
        zsc = ctx.enter_context(nc.sbuf_tensor("zsc", [128, 128], BF16))
        tgc = ctx.enter_context(nc.sbuf_tensor("tgc", [128, 128], F32))
        sq8 = ctx.enter_context(nc.sbuf_tensor("sq8", [128, 128], BF16))
        rstd8 = ctx.enter_context(nc.sbuf_tensor("rstd8", [1, 128], F32))
        rsb8 = ctx.enter_context(nc.sbuf_tensor("rsb8", [128, 128], F32))
        eps8 = ctx.enter_context(nc.sbuf_tensor("eps8", [1, 1], F32))

        s_x8 = Sem(nc, ctx, "s_x8")
        s_p8 = Sem(nc, ctx, "s_p8")
        s_t8 = Sem(nc, ctx, "s_t8")
        s_e8 = Sem(nc, ctx, "s_e8")
        s_y8 = Sem(nc, ctx, "s_y8")
        s_g8 = Sem(nc, ctx, "s_g8")

        @block.gpsimd
        def _(g):
            s_g8.inc(g.memset(eps8[:, :], RMS_EPS))

        for c in range(MCH):
            blk, cb = c // 4, c % 4

            @block.gpsimd
            def _(g, c=c):
                g.wait_ge(s_c7.h, carry_done)
                s_y8.wait(g)
                s_g8.wait(g)
                inst = g.dma_start(
                    out=x_sl[:, :].rearrange("p (a q) -> p a q", a=1),
                    in_=xact_dram.rearrange("c p ch t -> p c (ch t)")[:, c:c + 1, :])
                s_x8.inc(inst, 16)
                inst = g.dma_start(
                    out=z_sl[:, :].rearrange("p (a q) -> p a q", a=1),
                    in_=z_dram.rearrange("c p ch t -> p c (ch t)")[:, c:c + 1, :])
                s_x8.inc(inst, 16)
                inst = g.dma_start(
                    out=st_sb[:, :].rearrange("p (a q) -> p a q", a=1),
                    in_=states_dram.rearrange("c p q -> p c q")[:, c:c + 1, :])
                s_x8.inc(inst, 16)
                crv = cumrel_dram.rearrange("b p q -> (b p) q")
                inst = g.dma_start(out=crv[0:64, :], in_=cum_sb[:, c * 128:(c + 1) * 128])
                s_x8.inc(inst, 16)
                inst = g.dma_start(out=crv[64:128, :], in_=lndtf_sb[:, c * 128:(c + 1) * 128])
                s_x8.inc(inst, 16)


            @block.vector
            def _(v, c=c):
                s_x8.wait(v)
                if c % 4 == 0:
                    inst = v.tensor_copy(prev2[:, :], carr[c // 4][:, :])
                    s_p8.inc(inst)
                s_p8.wait(v)
                inst = v.tensor_copy(prev2b[:, :], prev2[:, :])
                s_p8.inc(inst)

            emit_xtm(c, s_x8, s_x8.n)

            for h in range(MNH):
                g8, hh8 = h // 8, h % 8
                if hh8 == 0:
                    @block.gpsimd
                    def _(g, c=c, g8=g8):
                        s_t8.wait(g)
                        s_p8.wait(g)
                        crf = cumrel_dram.rearrange("b p q -> b (p q)")
                        inst = g.dma_start(out=cumR[:, :], in_=crf[0:1, g8 * 1024:(g8 + 1) * 1024])
                        s_x8.inc(inst, 16)
                        inst = g.dma_start(out=ldR[:, :], in_=crf[1:2, g8 * 1024:(g8 + 1) * 1024])
                        s_x8.inc(inst, 16)

                    @block.vector
                    def _(v, c=c, g8=g8):
                        s_x8.wait(v)
                        inst = v.tensor_scalar_mul(ncumR[:, :], cumR[:, :], -1.0)
                        s_p8.inc(inst)

                @block.scalar
                def _(s, c=c, h=h, hh8=hh8):
                    s_p8.wait(s)
                    s_t8.wait(s)
                    inst = s.activation(er_sb[:, :], cumR[0:1, hh8 * 128:(hh8 + 1) * 128], AF.Exp)
                    s_p8.inc(inst)

                @block.tensor
                def _(t, c=c, h=h):
                    s_p8.wait(t)
                    s_e8.wait(t)
                    inst = t.matmul(pd[4][0:64, 0:128], onesf[0:1, 0:64], er_sb[:, :],
                                    start=True, stop=True)
                    s_t8.inc(inst)

                @block.scalar
                def _(s, c=c, h=h):
                    s_t8.wait(s)
                    inst = s.activation(exb_sb[:, :], pd[4][0:64, 0:128], AF.Identity)
                    s_e8.inc(inst)

                @block.tensor
                def _(t, c=c, h=h, hh8=hh8):
                    s_p8.wait(t)
                    s_e8.wait(t)
                    t.matmul(pd[0][:, 0:128], ncumR[0:1, hh8 * 128:(hh8 + 1) * 128],
                             onesf[0:1, 0:128], start=True, stop=False)
                    t.matmul(pd[0][:, 0:128], onesf[0:1, 0:128],
                             cumR[0:1, hh8 * 128:(hh8 + 1) * 128], start=False, stop=False)
                    t.matmul(pd[0][:, 0:128], ldR[0:1, hh8 * 128:(hh8 + 1) * 128],
                             onesf[0:1, 0:128], start=False, stop=False)
                    inst = t.matmul(pd[0][:, 0:128], identb[:, :], segmask[:, :],
                                    start=False, stop=True)
                    s_t8.inc(inst)
                    s_x8.wait(t)
                    t.wait_ge(s_cv.h, bc_ready)
                    inst = t.matmul(pd[1][:, 0:128], bc_sb[:, c * 128:(c + 1) * 128],
                                    bc_sb[:, TPC + c * 128: TPC + (c + 1) * 128],
                                    start=True, stop=True)
                    s_t8.inc(inst)

                @block.scalar
                def _(s, c=c, h=h):
                    s_t8.wait(s)
                    s_y8.wait(s)
                    inst = s.activation(pdec_sb[:, :], pd[0][:, 0:128], AF.Exp)
                    s_e8.inc(inst)

                @block.vector
                def _(v, c=c, h=h):
                    s_e8.wait(v)
                    inst = v.tensor_mul(p_sb[:, :], pd[1][:, 0:128], pdec_sb[:, :])
                    s_p8.inc(inst)

                @block.tensor
                def _(t, c=c, h=h):
                    s_p8.wait(t)
                    s_y8.wait(t)
                    xoff = (h // 2) * 128 + (h % 2) * 64
                    inst = t.matmul(pd[2][0:64, 0:128], xtm_sb[:, xoff:xoff + 64], p_sb[:, :],
                                    start=True, stop=True)
                    s_t8.inc(inst)
                    inst = t.matmul(pd[3][0:64, 0:128], prev2b[:, h * 64:(h + 1) * 64],
                                    bc_sb[:, TPC + c * 128: TPC + (c + 1) * 128],
                                    start=True, stop=True)
                    s_t8.inc(inst)

                @block.vector
                def _(v, c=c, h=h):
                    s_t8.wait(v)
                    inst = v.tensor_mul(yoff_sb[:, :], pd[3][0:64, 0:128], exb_sb[:, :])
                    s_p8.inc(inst)
                    s_p8.wait(v)
                    ybase = (h % 2) * 64
                    inst = v.tensor_add(y_c[ybase:ybase + 64, (h // 2) * 128:(h // 2 + 1) * 128],
                                        pd[2][0:64, 0:128], yoff_sb[:, :])
                    s_y8.inc(inst)

            @block.vector
            def _(v, c=c):
                s_y8.wait(v)
                if c % 4 != 3:
                    for h in range(MNH):
                        s_p8.wait(v)
                        inst = v.scalar_tensor_tensor(
                            out=prev2[:, h * 64:(h + 1) * 64],
                            in0=prev2[:, h * 64:(h + 1) * 64],
                            scalar=aball[:, c * 64 + h: c * 64 + h + 1],
                            in1=st_sb[:, h * 64:(h + 1) * 64],
                            op0=OP.mult, op1=OP.add)
                        s_p8.inc(inst)

            @block.vector
            def _(v, c=c):
                s_y8.wait(v)
                for ch in range(32):
                    s_g8.wait(v)
                    inst = v.scalar_tensor_tensor(
                        out=y_c[:, ch * 128:(ch + 1) * 128],
                        in0=x_sl[:, ch * 128:(ch + 1) * 128],
                        scalar=dtil_sb[:, ch:ch + 1],
                        in1=y_c[:, ch * 128:(ch + 1) * 128],
                        op0=OP.mult, op1=OP.add)
                    s_g8.inc(inst)

            for ch in range(32):
                @block.scalar
                def _(s, c=c, ch=ch):
                    s_x8.wait(s)
                    s_g8.wait(s)
                    inst = s.activation(zsc[:, :], z_sl[:, ch * 128:(ch + 1) * 128], AF.Silu)
                    s_g8.inc(inst)

                @block.vector
                def _(v, c=c, ch=ch):
                    s_g8.wait(v)
                    inst = v.tensor_mul(tgc[:, :], y_c[:, ch * 128:(ch + 1) * 128], zsc[:, :])
                    s_g8.inc(inst)

                @block.scalar
                def _(s, c=c, ch=ch):
                    s_g8.wait(s)
                    s_t8.wait(s)
                    inst = s.activation(sq8[:, :], tgc[:, :], AF.Square)
                    s_g8.inc(inst)

                @block.tensor
                def _(t, c=c, ch=ch):
                    s_g8.wait(t)
                    if ch == 0:
                        s_e8.wait(t)
                    inst = t.matmul(pd[5][0:1, 0:128], onesc[:, :], sq8[:, :],
                                    start=(ch == 0), stop=(ch == 31))
                    s_t8.inc(inst)

            @block.vector
            def _(v, c=c):
                s_t8.wait(v)
                inst = v.tensor_scalar_mul(rstd8[:, :], pd[5][0:1, 0:128], 1.0 / DIN)
                s_g8.inc(inst)

            @block.scalar
            def _(s, c=c):
                s_g8.wait(s)
                inst = s.activation(rstd8[:, :], rstd8[:, :], AF.Sqrt, bias=eps8[:, :])
                s_g8.inc(inst)

            @block.vector
            def _(v, c=c):
                s_g8.wait(v)
                inst = v.reciprocal(rstd8[:, :], rstd8[:, :])
                s_g8.inc(inst)

            @block.tensor
            def _(t, c=c):
                s_g8.wait(t)
                s_e8.wait(t)
                inst = t.matmul(pd[4][:, 0:128], onesf[0:1, 0:128], rstd8[:, :],
                                start=True, stop=True)
                s_t8.inc(inst)

            @block.scalar
            def _(s, c=c):
                s_t8.wait(s)
                inst = s.activation(rsb8[:, :], pd[4][:, 0:128], AF.Identity)
                s_e8.inc(inst)

            for ch in range(32):
                @block.scalar
                def _(s, c=c, ch=ch):
                    s_e8.wait(s)
                    s_g8.wait(s)
                    inst = s.activation(zsc[:, :], z_sl[:, ch * 128:(ch + 1) * 128], AF.Silu)
                    s_g8.inc(inst)

                @block.vector
                def _(v, c=c, ch=ch):
                    s_g8.wait(v)
                    inst = v.tensor_mul(tgc[:, :], y_c[:, ch * 128:(ch + 1) * 128], zsc[:, :])
                    s_g8.inc(inst)
                    s_g8.wait(v)
                    inst = v.tensor_mul(tgc[:, :], tgc[:, :], rsb8[:, :])
                    s_g8.inc(inst)
                    s_g8.wait(v)
                    inst = v.tensor_scalar(
                        out=t_sb[:, ch * TPC + c * 128: ch * TPC + (c + 1) * 128],
                        in0=tgc[:, :],
                        scalar1=nw_sb[:, ch:ch + 1], scalar2=None, op0=OP.mult)
                    s_g8.inc(inst)
        t_done = s_g8.n

        wo_sb = [z_sl, xtm_sb]
        s_w9 = Sem(nc, ctx, "s_w9")
        s_m9 = Sem(nc, ctx, "s_m9")
        s_x9 = Sem(nc, ctx, "s_x9")
        s_o9 = Sem(nc, ctx, "s_o9")
        s_out = Sem(nc, ctx, "s_out")
        op_done = {}
        for o in range(KC):
            buf = o % 2

            @block.gpsimd
            def _(g, o=o, buf=buf):
                if o >= 2:
                    s_m9.wait(g, op_done[o - 2])
                else:
                    g.wait_ge(s_g8.h, t_done)
                for q in range(8):
                    src = w_out.rearrange("(kc p) o -> p kc o", p=128)[
                        :, q * 4:(q + 1) * 4, o * 128:(o + 1) * 128]
                    dst = wo_sb[buf][:, :].rearrange("p (kc o) -> p kc o", kc=32)[
                        :, q * 4:(q + 1) * 4, :]
                    inst = g.dma_start(out=dst, in_=src)
                    s_w9.inc(inst, 16)

            @block.sync
            def _(s, o=o):
                if o == 0:
                    s.wait_ge(s_g8.h, t_done)
                s_o9.wait(s)
                inst = s.dma_start(out=scrD[:, :], in_=x_in[o * 128:(o + 1) * 128, :])
                s_x9.inc(inst, 16)

            for tb in range(2):
                @block.tensor
                def _(t, o=o, buf=buf, tb=tb):
                    s_w9.wait(t)
                    t.wait_ge(s_g8.h, t_done)
                    s_o9.wait(t)
                    for k in range(32):
                        inst = t.matmul(pd[tb][:, :], wo_sb[buf][:, k * 128:(k + 1) * 128],
                                        t_sb[:, k * TPC + tb * 512: k * TPC + tb * 512 + 512],
                                        start=(k == 0), stop=(k == 31))
                    s_m9.inc(inst)
                    if tb == 1:
                        op_done[o] = s_m9.n

                @block.vector
                def _(v, o=o, tb=tb):
                    s_m9.wait(v)
                    s_x9.wait(v)
                    s_out.wait(v)
                    inst = v.tensor_add(y_c[:, tb * 512:(tb + 1) * 512], pd[tb][:, :],
                                        scrD[:, tb * 512:(tb + 1) * 512])
                    s_o9.inc(inst)

            @block.sync
            def _(s, o=o):
                s_o9.wait(s)
                inst = s.dma_start(out=x3_fm[o * 128:(o + 1) * 128, :], in_=y_c[:, 0:TPC])
                s_out.inc(inst, 16)

        if debug:
            dbg1 = nc.dram_tensor("dbg1", [4 * 64, TPC], F32, kind="ExternalOutput").ap()
            dbg2 = nc.dram_tensor("dbg2", [2 * 128, DIN], BF16, kind="ExternalOutput").ap()
            dbg3 = nc.dram_tensor("dbg3", [128, 2 * TPC], BF16, kind="ExternalOutput").ap()
            dbg4 = nc.dram_tensor("dbg4", [128, 32 * TPC], BF16, kind="ExternalOutput").ap()
            dbg5 = nc.dram_tensor("dbg5", [NC * 2 * 128, DIN], BF16, kind="ExternalOutput").ap()
            dbg6 = nc.dram_tensor("dbg6", [NC * 2, MNH], F32, kind="ExternalOutput").ap()
            @block.sync
            def _(s):
                s_out.wait(s)
                inst = s.dma_start(out=dbg5, in_=summ_g.rearrange("s p q -> (s p) q"))
                s_out.inc(inst, 16)
                inst = s.dma_start(out=dbg6, in_=la_g)
                s_out.inc(inst, 16)
                inst = s.dma_start(out=dbg1[0:64, :], in_=dt_sb[:, :])
                s_out.inc(inst, 16)
                inst = s.dma_start(out=dbg1[64:128, :], in_=dtf_sb[:, :])
                s_out.inc(inst, 16)
                inst = s.dma_start(out=dbg1[128:192, :], in_=cum_sb[:, :])
                s_out.inc(inst, 16)
                inst = s.dma_start(out=dbg1[192:256, 0:MCH], in_=cl_sb[:, :])
                s_out.inc(inst, 16)
                inst = s.dma_start(out=dbg2[0:128, :], in_=carr[0][:, :])
                s_out.inc(inst, 16)
                inst = s.dma_start(out=dbg2[128:256, :], in_=carr[1][:, :])
                s_out.inc(inst, 16)
                inst = s.dma_start(out=dbg3, in_=bc_sb[:, :])
                s_out.inc(inst, 16)
                inst = s.dma_start(out=dbg4, in_=t_sb[:, :])
                s_out.inc(inst, 16)

        @block.sync
        def _(s):
            s_out.wait(s)

    return nc



QEPS = 1e-8
I32 = mybir.dt.int32


def build_E(debug=False, stop_after=None):
    nc = bass.Bass("TRN2", target_bir_lowering=False)
    x_tm = nc.dram_tensor("x3_tm", [TPC, D], F32, kind="ExternalInput").ap()
    x_fm = nc.dram_tensor("x3_fm", [D, TPC], F32, kind="ExternalInput").ap()
    w_g = nc.dram_tensor("w_g", [H_FFN, D], F32, kind="ExternalInput").ap()
    w_u = nc.dram_tensor("w_u", [H_FFN, D], F32, kind="ExternalInput").ap()
    w_d = nc.dram_tensor("w_d", [D, H_FFN], F32, kind="ExternalInput").ap()
    x4_fm = nc.dram_tensor("x4_fm", [D, TPC], F32, kind="ExternalOutput").ap()
    rs_dram = nc.dram_tensor("rs_dramE", [8, 128], F32, kind="Internal").ap()

    with ExitStack() as ctx:
        n_fm = ctx.enter_context(nc.sbuf_tensor("n_fm", [128, KC * TPC], BF16))
        wg_sb = [ctx.enter_context(nc.sbuf_tensor(f"wgE{i}", [128, KC * 128], BF16)) for i in range(2)]
        wu_sb = [ctx.enter_context(nc.sbuf_tensor(f"wuE{i}", [128, KC * 128], BF16)) for i in range(2)]
        wd_sb = [ctx.enter_context(nc.sbuf_tensor(f"wdE{i}", [128, 1024], BF16)) for i in range(2)]
        identb = ctx.enter_context(nc.sbuf_tensor("identbE", [128, 128], BF16))
        identf = ctx.enter_context(nc.sbuf_tensor("identfE", [128, 128], F32))
        onesf = ctx.enter_context(nc.sbuf_tensor("onesfE", [1, 128], F32))
        stat = ctx.enter_context(nc.sbuf_tensor("statE", [128, 4], F32))
        rs_all = ctx.enter_context(nc.sbuf_tensor("rsallE", [128, 8], F32))
        epsq = ctx.enter_context(nc.sbuf_tensor("epsqE", [128, 1], F32))
        r8 = ctx.enter_context(nc.sbuf_tensor("r8E", [8, 128], F32))
        srow = ctx.enter_context(nc.sbuf_tensor("srowE", [1, TPC], F32))
        rsb = ctx.enter_context(nc.sbuf_tensor("rsbE", [128, TPC], F32))
        scratchE = ctx.enter_context(nc.sbuf_tensor("scratchE", [128, TPC], F32))
        sil = scratchE[:, 0:512]
        xr_sb = scratchE[:, 512:1024]
        o_sb = scratchE[:, 0:512]
        e_g = ctx.enter_context(nc.sbuf_tensor("egE", [128, 512], F32))

        s_cast = Sem(nc, ctx, "s_castE")
        s_tr = Sem(nc, ctx, "s_trE")
        s_st = Sem(nc, ctx, "s_stE")
        s_qx = Sem(nc, ctx, "s_qx")
        s_q = Sem(nc, ctx, "s_q")
        s_qt = Sem(nc, ctx, "s_qt")
        s_qe = Sem(nc, ctx, "s_qe")
        s_w = Sem(nc, ctx, "s_wE")
        s_g = Sem(nc, ctx, "s_gE")
        s_u = Sem(nc, ctx, "s_uE")
        s_gu = Sem(nc, ctx, "s_guE")
        s_wd = Sem(nc, ctx, "s_wdE")
        s_dn = Sem(nc, ctx, "s_dnE")
        s_x = Sem(nc, ctx, "s_xE")
        s_o = Sem(nc, ctx, "s_oE")
        s_out = Sem(nc, ctx, "s_outE")
        sems = {"cast": s_cast, "tr": s_tr, "st": s_st}
        block = ctx.enter_context(nc.Block())

        @block.gpsimd
        def _(g):
            make_identity(nc, identb[:, :], s_q)
            make_identity(nc, identf[:, :], s_q)
            s_q.inc(g.memset(onesf[:, :], 1.0))
            s_q.inc(g.memset(epsq[:, :], QEPS))

        sQ = ExitStack()
        e_x = sQ.enter_context(nc.sbuf_tensor("exE", [128, D], F32))
        e_y = sQ.enter_context(nc.sbuf_tensor("eyE", [128, D], F32))
        e_c = sQ.enter_context(nc.sbuf_tensor("ecE", [128, D], F32))
        e_i = sQ.enter_context(nc.sbuf_tensor("eiE", [128, D], I32))
        n_tm = sQ.enter_context(nc.sbuf_tensor("ntmE", [128, D], BF16))
        pe = sQ.enter_context(nc.psum_tensor("peE", [128, 512], BF16))
        pf = sQ.enter_context(nc.psum_tensor("pfE", [128, 128], F32))

        for k in range(8):
            @block.sync
            def _(s, k=k):
                s_q.wait(s)
                s_qt.wait(s)
                inst = s.dma_start(out=e_x[:, :], in_=x_tm[k * 128:(k + 1) * 128, :])
                s_qx.inc(inst, 16)

            @block.vector
            def _(v, k=k):
                s_qx.wait(v)
                inst = v.tensor_reduce(stat[:, 0:1], e_x[:, :], axis=mybir.AxisListType.X, op=OP.add)
                s_q.inc(inst)
                s_q.wait(v)
                inst = v.tensor_scalar_mul(stat[:, 0:1], stat[:, 0:1], 1.0 / D)
                s_q.inc(inst)
                s_q.wait(v)
                inst = v.tensor_scalar(out=e_x[:, :], in0=e_x[:, :], scalar1=stat[:, 0:1],
                                       scalar2=None, op0=OP.subtract)
                s_q.inc(inst)

            @block.scalar
            def _(s, k=k):
                s_q.wait(s)
                inst = s.activation(e_y[:, :], e_x[:, :], AF.Square)
                s_q.inc(inst)

            @block.vector
            def _(v, k=k):
                s_q.wait(v)
                inst = v.tensor_reduce(stat[:, 1:2], e_y[:, :], axis=mybir.AxisListType.X, op=OP.add)
                s_q.inc(inst)
                s_q.wait(v)
                inst = v.tensor_scalar_mul(stat[:, 1:2], stat[:, 1:2], 1.0 / D)
                s_q.inc(inst)

            @block.scalar
            def _(s, k=k):
                s_q.wait(s)
                inst = s.activation(stat[:, 1:2], stat[:, 1:2], AF.Sqrt, bias=epsq[:, :])
                s_q.inc(inst)

            @block.vector
            def _(v, k=k):
                s_q.wait(v)
                inst = v.reciprocal(stat[:, 1:2], stat[:, 1:2])
                s_q.inc(inst)
                s_q.wait(v)
                inst = v.tensor_scalar(out=e_x[:, :], in0=e_x[:, :], scalar1=stat[:, 1:2],
                                       scalar2=None, op0=OP.mult)
                s_q.inc(inst)

            @block.scalar
            def _(s, k=k):
                s_q.wait(s)
                inst = s.activation(e_y[:, :], e_x[:, :], AF.Abs)
                s_q.inc(inst)

            @block.vector
            def _(v, k=k):
                s_q.wait(v)
                inst = v.tensor_reduce(stat[:, 2:3], e_y[:, :], axis=mybir.AxisListType.X, op=OP.max)
                s_q.inc(inst)
                s_q.wait(v)
                inst = v.reciprocal(stat[:, 3:4], stat[:, 2:3])
                s_q.inc(inst)
                s_q.wait(v)
                inst = v.tensor_scalar_mul(stat[:, 3:4], stat[:, 3:4], 127.0)
                s_q.inc(inst)
                s_q.wait(v)
                inst = v.tensor_scalar_mul(rs_all[:, k:k + 1], stat[:, 2:3], 1.0 / 127.0)
                s_q.inc(inst)
                s_q.wait(v)
                inst = v.tensor_scalar(out=e_y[:, :], in0=e_x[:, :], scalar1=stat[:, 3:4],
                                       scalar2=None, op0=OP.mult)
                s_q.inc(inst)
                s_q.wait(v)
                inst = v.tensor_copy(e_i[:, :], e_y[:, :])
                s_q.inc(inst)
                s_q.wait(v)
                inst = v.tensor_copy(e_x[:, :], e_i[:, :])
                s_q.inc(inst)
                s_q.wait(v)
                inst = v.tensor_sub(e_y[:, :], e_y[:, :], e_x[:, :])
                s_q.inc(inst)
                s_q.wait(v)
                inst = v.tensor_scalar(out=e_c[:, :], in0=e_y[:, :], scalar1=0.5,
                                       scalar2=None, op0=OP.is_ge)
                s_q.inc(inst)
                s_q.wait(v)
                inst = v.tensor_add(e_x[:, :], e_x[:, :], e_c[:, :])
                s_q.inc(inst)
                s_q.wait(v)
                inst = v.tensor_scalar(out=e_c[:, :], in0=e_y[:, :], scalar1=-0.5,
                                       scalar2=None, op0=OP.is_le)
                s_q.inc(inst)
                s_q.wait(v)
                inst = v.tensor_sub(e_x[:, :], e_x[:, :], e_c[:, :])
                s_q.inc(inst)
                s_q.wait(v)
                inst = v.tensor_scalar(out=n_tm[:, :], in0=e_x[:, :], scalar1=-128.0,
                                       scalar2=127.0, op0=OP.max, op1=OP.min)
                s_q.inc(inst)

            for g4 in range(4):
                @block.tensor
                def _(t, k=k, g4=g4):
                    s_q.wait(t)
                    s_qe.wait(t)
                    for j in range(4):
                        kc = g4 * 4 + j
                        inst = t.transpose(pe[:, j * 128:(j + 1) * 128],
                                           n_tm[:, kc * 128:(kc + 1) * 128], identb[:, :])
                    s_qt.inc(inst)

                @block.scalar
                def _(s, k=k, g4=g4):
                    s_qt.wait(s)
                    for j in range(4):
                        kc = g4 * 4 + j
                        inst = s.activation(
                            n_fm[:, kc * TPC + k * 128: kc * TPC + (k + 1) * 128],
                            pe[:, j * 128:(j + 1) * 128], AF.Identity)
                    s_qe.inc(inst)

        if stop_after == "q0":
            sQ.close()
            qe_n = s_qe.n
            @block.sync
            def _(s):
                s.wait_ge(s_qe.h, qe_n)
                if debug:
                    dbgq = nc.dram_tensor("dbgq", [128, KC * TPC], BF16, kind="ExternalOutput").ap()
                    inst = s.dma_start(out=dbgq, in_=n_fm[:, :])
                    s_out.inc(inst, 16)
            @block.sync
            def _(s):
                s_out.wait(s)
            return nc

        @block.tensor
        def _(t):
            s_q.wait(t)
            s_qe.wait(t)
            inst = t.transpose(pf[0:8, 0:128], rs_all[:, :], identf[:, :])
            s_qt.inc(inst)

        @block.scalar
        def _(s):
            s_qt.wait(s)
            inst = s.activation(r8[:, :], pf[0:8, 0:128], AF.Identity)
            s_qe.inc(inst)

        @block.gpsimd
        def _(g):
            s_qe.wait(g)
            inst = g.dma_start(out=rs_dram, in_=r8[:, :])
            s_qx.inc(inst, 16)
            g.wait_ge(s_qx.h, s_qx.n)
            inst = g.dma_start(out=srow[:, :], in_=rs_dram.rearrange("(x a) b -> x (a b)", x=1))
            s_qx.inc(inst, 16)

        if stop_after == "q1":
            sQ.close()
            qx_n = s_qx.n
            @block.sync
            def _(s):
                s.wait_ge(s_qx.h, qx_n)
                if debug:
                    dbgs = nc.dram_tensor("dbgs", [1, TPC], F32, kind="ExternalOutput").ap()
                    inst = s.dma_start(out=dbgs, in_=srow[:, :])
                    s_out.inc(inst, 16)
            @block.sync
            def _(s):
                s_out.wait(s)
            return nc

        sQ.close()
        gu = ctx.enter_context(nc.sbuf_tensor("guE", [128, HC * TPC], BF16))
        pool = [ctx.enter_context(nc.psum_tensor(f"ppE{i}", [128, 512], F32)) for i in range(8)]

        if stop_after == "q2a":
            qx_n = s_qx.n
            @block.sync
            def _(s):
                s.wait_ge(s_qx.h, qx_n)
            @block.sync
            def _(s):
                s_out.wait(s)
            return nc

        for tb in range(2):
            @block.tensor
            def _(t, tb=tb):
                s_qx.wait(t)
                s_qe.wait(t)
                for i in range(4):
                    inst = t.matmul(pool[i][:, 0:128], onesf[0:1, 0:128],
                                    srow[0:1, tb * 512 + i * 128: tb * 512 + (i + 1) * 128],
                                    start=True, stop=True)
                    s_qt.inc(inst)

            @block.scalar
            def _(s, tb=tb):
                s_qt.wait(s)
                for i in range(4):
                    inst = s.activation(rsb[:, tb * 512 + i * 128: tb * 512 + (i + 1) * 128],
                                        pool[i][:, 0:128], AF.Identity)
                    s_qe.inc(inst)
        n_done = s_qe.n

        if stop_after == "q2":
            @block.sync
            def _(s):
                s.wait_ge(s_qe.h, n_done)
                if debug:
                    dbgr = nc.dram_tensor("dbgr", [128, TPC], F32, kind="ExternalOutput").ap()
                    inst = s.dma_start(out=dbgr, in_=rsb[:, :])
                    s_out.inc(inst, 16)
            @block.sync
            def _(s):
                s_out.wait(s)
            return nc

        if stop_after == "quant":
            if debug:
                dbgq = nc.dram_tensor("dbgq", [128, KC * TPC], BF16, kind="ExternalOutput").ap()
                dbgs = nc.dram_tensor("dbgs", [1, TPC], F32, kind="ExternalOutput").ap()
                @block.sync
                def _(s):
                    s.wait_ge(s_qe.h, n_done)
                    inst = s.dma_start(out=dbgq, in_=n_fm[:, :])
                    s_out.inc(inst, 16)
                    inst = s.dma_start(out=dbgs, in_=srow[:, :])
                    s_out.inc(inst, 16)
            @block.sync
            def _(s):
                s_out.wait(s)
            return nc

        staged = emit_weight_stage(nc, block, ctx, sems, [
            (w_g, True, 2), (w_u, True, 2), (w_d, True, 2)], tag="E")
        wgT, wuT, wdT = staged[0][1], staged[1][1], staged[2][1]
        stage_done = s_st.n

        if stop_after == "stage":
            @block.gpsimd
            def _(g):
                g.wait_ge(s_st.h, stage_done)
            @block.sync
            def _(s):
                s_out.wait(s)
            return nc

        gu_mm_done = {}
        for hh in range(HC):
            buf = hh % 2

            @block.gpsimd
            def _(g, hh=hh, buf=buf):
                if hh >= 2:
                    s_g.wait(g, gu_mm_done[hh - 2])
                else:
                    g.wait_ge(s_st.h, stage_done)
                for (wT, dst) in ((wgT, wg_sb[buf]), (wuT, wu_sb[buf])):
                    for q in range(4):
                        src = wT.rearrange("(kc p) h -> p kc h", p=128)[
                            :, q * 4:(q + 1) * 4, hh * 128:(hh + 1) * 128]
                        dstv = dst[:, :].rearrange("p (kc o) -> p kc o", kc=KC)[
                            :, q * 4:(q + 1) * 4, :]
                        inst = g.dma_start(out=dstv, in_=src)
                        s_w.inc(inst, 16)

            for tb in range(2):
                @block.tensor
                def _(t, hh=hh, buf=buf, tb=tb):
                    s_w.wait(t)
                    if hh == 0 and tb == 0:
                        t.wait_ge(s_qe.h, n_done)
                    s_gu.wait(t)
                    for k in range(KC):
                        t.matmul(pool[5][:, :], wg_sb[buf][:, k * 128:(k + 1) * 128],
                                 n_fm[:, k * TPC + tb * 512: k * TPC + tb * 512 + 512],
                                 start=(k == 0), stop=(k == KC - 1))
                    for k in range(KC):
                        inst = t.matmul(pool[6][:, :], wu_sb[buf][:, k * 128:(k + 1) * 128],
                                        n_fm[:, k * TPC + tb * 512: k * TPC + tb * 512 + 512],
                                        start=(k == 0), stop=(k == KC - 1))
                    s_g.inc(inst)
                    if tb == 1:
                        gu_mm_done[hh] = s_g.n

                @block.vector
                def _(v, hh=hh, tb=tb):
                    s_g.wait(v)
                    s_u.wait(v)
                    inst = v.tensor_mul(e_g[:, :], pool[5][:, :], rsb[:, tb * 512:(tb + 1) * 512])
                    s_u.inc(inst)

                @block.scalar
                def _(s, hh=hh, tb=tb):
                    s_u.wait(s)
                    s_gu.wait(s)
                    inst = s.activation(sil, e_g[:, :], AF.Silu)
                    s_u.inc(inst)

                @block.vector
                def _(v, hh=hh, tb=tb):
                    s_u.wait(v)
                    inst = v.tensor_mul(e_g[:, :], pool[6][:, :], rsb[:, tb * 512:(tb + 1) * 512])
                    s_u.inc(inst)
                    s_u.wait(v)
                    inst = v.tensor_mul(gu[:, hh * TPC + tb * 512: hh * TPC + tb * 512 + 512],
                                        e_g[:, :], sil)
                    s_gu.inc(inst)
        gu_done = s_gu.n

        if stop_after == "gate":
            if debug:
                dbgu = nc.dram_tensor("dbgu", [128, HC * TPC], BF16, kind="ExternalOutput").ap()
                @block.sync
                def _(s):
                    s.wait_ge(s_gu.h, gu_done)
                    inst = s.dma_start(out=dbgu, in_=gu[:, :])
                    s_out.inc(inst, 16)
            @block.sync
            def _(s):
                s_out.wait(s)
            return nc

        for tb in range(2):
            for dh in range(2):
                dn_mm_done = {}
                grp_start = s_dn.n
                for hh in range(HC):
                    buf = hh % 2

                    @block.gpsimd
                    def _(g, tb=tb, dh=dh, hh=hh, buf=buf, grp_start=grp_start):
                        if hh >= 2:
                            s_dn.wait(g, dn_mm_done[hh - 2])
                        else:
                            s_dn.wait(g, grp_start)
                        inst = g.dma_start(out=wd_sb[buf][:, :],
                                           in_=wdT[hh * 128:(hh + 1) * 128,
                                                   dh * 1024:(dh + 1) * 1024])
                        s_wd.inc(inst, 16)

                    @block.tensor
                    def _(t, tb=tb, dh=dh, hh=hh, buf=buf):
                        s_wd.wait(t)
                        if hh == 0:
                            s_gu.wait(t, gu_done)
                            s_o.wait(t)
                        for d in range(8):
                            inst = t.matmul(pool[d][:, :], wd_sb[buf][:, d * 128:(d + 1) * 128],
                                            gu[:, hh * TPC + tb * 512: hh * TPC + tb * 512 + 512],
                                            start=(hh == 0), stop=(hh == HC - 1))
                        s_dn.inc(inst)
                        dn_mm_done[hh] = s_dn.n

                for d in range(8):
                    @block.sync
                    def _(s, tb=tb, dh=dh, d=d):
                        s_gu.wait(s, gu_done)
                        s_o.wait(s)
                        inst = s.dma_start(out=xr_sb,
                                           in_=x_fm[(dh * 8 + d) * 128:(dh * 8 + d + 1) * 128,
                                                    tb * 512:(tb + 1) * 512])
                        s_x.inc(inst, 16)

                    @block.vector
                    def _(v, tb=tb, dh=dh, d=d):
                        s_dn.wait(v)
                        s_x.wait(v)
                        s_out.wait(v)
                        inst = v.tensor_add(o_sb, pool[d][:, :], xr_sb)
                        s_o.inc(inst)

                    @block.sync
                    def _(s, tb=tb, dh=dh, d=d):
                        s_o.wait(s)
                        inst = s.dma_start(
                            out=x4_fm[(dh * 8 + d) * 128:(dh * 8 + d + 1) * 128,
                                      tb * 512:(tb + 1) * 512],
                            in_=o_sb)
                        s_out.inc(inst, 16)

        if debug:
            dbgq = nc.dram_tensor("dbgq", [128, KC * TPC], BF16, kind="ExternalOutput").ap()
            dbgs = nc.dram_tensor("dbgs", [1, TPC], F32, kind="ExternalOutput").ap()
            dbgu = nc.dram_tensor("dbgu", [128, HC * TPC], BF16, kind="ExternalOutput").ap()

            @block.sync
            def _(s):
                s_out.wait(s)
                inst = s.dma_start(out=dbgq, in_=n_fm[:, :])
                s_out.inc(inst, 16)
                inst = s.dma_start(out=dbgs, in_=srow[:, :])
                s_out.inc(inst, 16)
                inst = s.dma_start(out=dbgu, in_=gu[:, :])
                s_out.inc(inst, 16)

        @block.sync
        def _(s):
            s_out.wait(s)

    return nc



def _blk_owner2(g):
    return (g, 0) if g < NC else (15 - g, 1)


def assemble_seq(per_core, axis):
    parts = []
    for g in range(16):
        c, pos = _blk_owner(g)
        idx = [slice(None)] * per_core[c].ndim
        idx[axis] = slice(pos * BLK, (pos + 1) * BLK)
        parts.append(per_core[c][tuple(idx)])
    return np.concatenate(parts, axis=axis)


def _core_slice_tm(arr_t_first, c):
    g0, g1 = CORE_BLOCKS[c]
    return np.ascontiguousarray(np.concatenate(
        [arr_t_first[g0 * BLK:(g0 + 1) * BLK], arr_t_first[g1 * BLK:(g1 + 1) * BLK]], 0))


_SLAB2BLK = [b for r in range(NC) for b in (r, 15 - r)]


def kernel(**inputs):
    I = {k: np.asarray(v) for k, v in inputs.items()}
    x = np.ascontiguousarray(I["L0_attn_x"][0], dtype=np.float32)

    ncA = build_A()
    imsA = [{"x_in": _core_slice_tm(x, c), "w_qkv": I["L0_attn_W_qkv"],
             "b_qkv": I["L0_attn_b_qkv"]} for c in range(NC)]
    resA = run_bass_kernel_spmd(ncA, imsA, core_ids=list(range(NC))).results

    q_g = np.concatenate([r["q_fm"] for r in resA], 0)
    k_g = np.concatenate([r["k_fm"] for r in resA], 0)
    v_g = np.concatenate([np.asarray(r["v_tm"]).reshape(NH * TPC, HD) for r in resA], 0)
    qn2_g = np.concatenate([r["qn2"] for r in resA], 0)
    kn2_g = np.concatenate([r["kn2"] for r in resA], 0)
    ncB = build_B()
    imsB = [{"q_g": q_g, "k_g": k_g, "v_g": v_g, "qn2_g": qn2_g, "kn2_g": kn2_g,
             "x_fm": resA[c]["x_fm"], "w_proj": I["L0_attn_W_proj"],
             "b_proj": I["L0_attn_b_proj"]} for c in range(NC)]
    resB = run_bass_kernel_spmd(ncB, imsB, core_ids=list(range(NC))).results

    ncC = build_C()
    imsC = [{"x1_fm": resB[c]["x1_fm"], "w_gate": I["L0_ffn_w_gate"],
             "w_up": I["L0_ffn_w_up"], "w_down": I["L0_ffn_w_down"]}
            for c in range(NC)]
    resC = run_bass_kernel_spmd(ncC, imsC, core_ids=list(range(NC))).results

    d_tiled = np.repeat(I["L1_attn_D"].reshape(32, 2), 64, axis=1).T
    d_tiled = np.ascontiguousarray(d_tiled.astype(np.float32))
    cmask2 = np.zeros((64, 256), np.float32)
    for m in range(16):
        for kk in range(16):
            if _SLAB2BLK[kk] <= _SLAB2BLK[m]:
                cmask2[:, m * 16 + kk] = 1.0
    imsD = []
    for c in range(NC):
        gq = CORE_BLOCKS[c]
        cmask = np.zeros((2, 64, 16), np.float32)
        for j, g in enumerate(gq):
            for m in range(16):
                if _SLAB2BLK[m] < g:
                    cmask[j, :, m] = 1.0
        tscale = np.ones((128, 2), np.float32)
        if gq[0] == 0:
            tscale[:, 0] = 0.0
        imsD.append({
            "x2_fm": resC[c]["x2_fm"], "w_in": I["L1_attn_W_in"],
            "conv_w": I["L1_attn_conv_w"], "conv_b": I["L1_attn_conv_b"],
            "dt_bias": I["L1_attn_dt_bias"], "a_log": I["L1_attn_A_log"],
            "d_tiled": d_tiled, "norm_w": I["L1_attn_norm_w"],
            "w_out": I["L1_attn_W_out"], "cmask": cmask, "cmask2": cmask2,
            "tscale": tscale})
    ncD = build_D()
    resD = run_bass_kernel_spmd(ncD, imsD, core_ids=list(range(NC))).results

    ncE = build_E()
    imsE = []
    for c in range(NC):
        x3_fm = np.asarray(resD[c]["x3_fm"])
        imsE.append({"x3_tm": np.ascontiguousarray(x3_fm.T), "x3_fm": x3_fm,
                     "w_g": I["L1_ffn_W_g"], "w_u": I["L1_ffn_W_u"],
                     "w_d": I["L1_ffn_W_d"]})
    resE = run_bass_kernel_spmd(ncE, imsE, core_ids=list(range(NC))).results

    out = np.empty((T, D), np.float32)
    for g in range(16):
        c, pos = _blk_owner(g)
        out[g * BLK:(g + 1) * BLK, :] =             np.asarray(resE[c]["x4_fm"])[:, pos * BLK:(pos + 1) * BLK].T
    return out.reshape(1, T, D)
